# version 13
# baseline (speedup 1.0000x reference)
import os
import numpy as np
import ml_dtypes
import concourse.bass as bass
import concourse.mybir as mybir
from concourse.bass_utils import run_bass_kernel_spmd

F32 = mybir.dt.float32
BF16 = mybir.dt.bfloat16
I32 = mybir.dt.int32
AF = mybir.ActivationFunctionType
OP = mybir.AluOpType
AX = mybir.AxisListType

D = 1024
DH = 4096
NCD = D // 128
NCH = DH // 128
TW = 256
EPS_LN = 1e-5
EPS_Z = 1e-6
QCLIP = 1e-5

ENGINES = ("sync", "tensor", "vector", "scalar", "gpsimd")
_CACHE = {}


class Sched:
    def __init__(self, mode="fast"):
        if mode is True:
            mode = "paranoid"
        elif mode is False:
            mode = "fast"
        self.mode = mode
        self.ops = {e: [] for e in ENGINES}
        self.cnt = {}
        self.strict = set()

    def emit(self, eng, fn, waits=(), inc=1, dma=False, strict=False,
             stream=None):
        sem = (stream if stream else eng + "_dma") if dma else eng
        w = []
        for t in waits:
            if t is None:
                continue
            ts, tv = t
            if (self.mode == "fast" and ts == eng
                    and not ts.endswith("_dma")
                    and (ts, tv) not in self.strict):
                continue
            assert tv <= self.cnt.get(ts, 0), f"future wait {t}"
            w.append((ts, tv))
        if self.mode == "paranoid" and self.cnt.get(eng, 0) > 0:
            w.append((eng, self.cnt[eng]))
        self.ops[eng].append((w, fn, sem, inc))
        self.cnt[sem] = self.cnt.get(sem, 0) + inc
        tok = (sem, self.cnt[sem])
        if strict:
            self.strict.add(tok)
        return tok


def build_program(S, dumps=(), paranoid=False):
    NT = S // TW
    NSC = S // 128
    nc = bass.Bass("TRN2", target_bir_lowering=False, debug=False,
                   detect_race_conditions=(paranoid == "paranoid"
                                           or paranoid is True))

    d_xT = nc.dram_tensor("xT", [D, S], F32, kind="ExternalInput").ap()
    d_w = {nm: nc.dram_tensor(nm, [D, D], F32, kind="ExternalInput").ap()
           for nm in ("wq", "wk", "wv", "wo")}
    d_b = {nm: nc.dram_tensor(nm, [1, D], F32, kind="ExternalInput").ap()
           for nm in ("bq", "bk", "bv", "bo")}
    d_w1t = nc.dram_tensor("w1t", [D, DH], F32, kind="ExternalInput").ap()
    d_cw = nc.dram_tensor("cw", [3, DH], F32, kind="ExternalInput").ap()
    d_cb = nc.dram_tensor("cb", [1, DH], F32, kind="ExternalInput").ap()
    d_w2t = nc.dram_tensor("w2t", [DH, D], F32, kind="ExternalInput").ap()
    d_yT = nc.dram_tensor("yT", [D, S], F32, kind="ExternalOutput").ap()
    d_xsp = nc.dram_tensor("xsp", [D, S], F32, kind="Internal").ap()
    d_m1q = nc.dram_tensor("m1q", [8 * 8 * 128, 512], BF16, kind="Internal").ap()
    d_m2q = nc.dram_tensor("m2q", [8 * 128, 32 * 128], BF16,
                           kind="Internal").ap()
    d_ident = nc.inline_tensor(np.eye(128, dtype=np.float32), name="identc").ap()
    selc = np.zeros((16, 1024), dtype=np.float32)
    for c in range(8):
        selc[2 * c, c * 128:c * 128 + 64] = 1.0
        selc[2 * c + 1, c * 128 + 64:c * 128 + 128] = 1.0
    d_sel = nc.inline_tensor(selc, name="selc").ap()

    dump_outs = {}

    A = nc.alloc_sbuf_tensor
    rx = A("RX", [128, 65536], mybir.dt.uint8)
    rw = A("RW", [128, 65536], mybir.dt.uint8)
    RYC = 28832
    ry = A("RY", [128, RYC], mybir.dt.uint8)
    rr = A("RR", [128, 10272], mybir.dt.uint8)
    ws_r = A("WS", [128, 2048], mybir.dt.uint8)

    def place(backing, cap, name, shape, dtype, off):
        h = nc.virtual_sbuf_tensor(name, shape, dtype)
        nc._tensor(h.name, list(h.shape), dtype, type="SB")
        m = nc.lookup_mloc(h)
        m.allocated = not nc.target_bir_lowering
        m.addr = nc.lookup_mloc(backing).addr + off
        nbytes = int(np.prod(shape[1:])) * mybir.dt.size(dtype)
        assert off + nbytes <= cap, (name, off, nbytes)
        return h

    x_f = place(rx, 65536, "x_f", [128, NCD * S], F32, 0)
    hs = place(rx, 65536, "hs", [128, NCH * TW], F32, 0)
    n2 = place(rx, 65536, "n2", [128, NCH * TW], BF16, 32768)
    xfread = place(rx, 65536, "xfread", [128, NCD * TW], F32, 49152)
    h_in = [place(rx, 65536, f"h_in{i}", [128, 258], F32, 57344 + i * 1056)
            for i in range(2)]
    rqb = place(rx, 65536, "rqb", [128, 258], F32, 59456)
    sBb = place(rx, 65536, "sBb", [128, TW], F32, 60512)
    rs2b = place(rx, 65536, "rs2b", [128, TW], F32, 61568)
    M2 = place(rx, 65536, "M2", [128, TW], F32, 62624)
    gbuf = place(rx, 65536, "gbuf", [128, TW], F32, 63680)

    wA_hi = place(rw, 65536, "wA_hi", [128, NCD * D], BF16, 0)
    wA_lo = place(rw, 65536, "wA_lo", [128, NCD * D], BF16, 16384)
    wB_hi = place(rw, 65536, "wB_hi", [128, NCD * D], BF16, 32768)
    wB_lo = place(rw, 65536, "wB_lo", [128, NCD * D], BF16, 49152)
    m1r = [place(rw, 65536, f"m1r{i}", [128, 4096], BF16, i * 8192)
           for i in range(2)]
    m2r = [place(rw, 65536, f"m2r{i}", [128, 4096], BF16, 16384 + i * 8192)
           for i in range(2)]
    n1ring = [place(rw, 65536, f"n1ring{i}", [128, NCD * 258], BF16,
                    32768 + i * 4160) for i in range(3)]
    xqread = place(rw, 65536, "xqread", [128, NCD * TW], F32, 45248)
    otile = place(rw, 65536, "otile", [128, 512], F32, 53440)
    mbq2 = place(rw, 65536, "mbq2", [128, 1024], BF16, 55488)

    kp = place(ry, RYC, "kp", [128, 1024], F32, 0)
    vp = place(ry, RYC, "vp", [128, 1024], F32, 4096)
    qsb = place(ry, RYC, "qsb", [128, NCD * TW], F32, 0)
    xt_sub = place(ry, RYC, "xt_sub", [128, D], F32, 0)
    tq = place(ry, RYC, "tq", [128, D], F32, 4096)
    sim_safe = bool(int(os.environ.get("K_SIM_SAFE", "0")))
    ttmp = place(ry, RYC, "ttmp", [128, TW], F32, 8192)
    n1t_f = place(ry, RYC, "n1t_f", [128, D], F32, 8192)
    n1t_i = (A("n1t_i", [128, D], I32) if sim_safe else
             place(ry, RYC, "n1t_i", [128, D], I32, 8192))
    etmp = place(ry, RYC, "etmp", [128, 1024], F32, 9216)
    fftmp = place(ry, RYC, "fftmp", [128, 512], F32, 12288)
    fft_i = (A("fft_i", [128, 256], I32) if sim_safe else
             place(ry, RYC, "fft_i", [128, 256], I32, 13312))
    gbuf2 = place(ry, RYC, "gbuf2", [128, TW], F32, 14336)
    tanh2 = place(ry, RYC, "tanh2", [128, TW], F32, 15392)
    kvacc = place(ry, RYC, "kvacc", [128, 520], F32, 13312)
    rzT = place(ry, RYC, "rzT", [16, TW], F32, 15392)
    tmprow = place(ry, RYC, "tmprow", [32, 128], F32, 15392)
    tmp32 = place(ry, RYC, "tmp32", [32, 128], F32, 15904)
    rz_t = place(ry, RYC, "rz_t", [128, 32], F32, 16416)
    xn_hi = place(ry, RYC, "xn_hi", [128, NCD * TW], BF16, 16544)
    xn_lo = place(ry, RYC, "xn_lo", [128, NCD * TW], BF16, 20640)
    attn_hi = place(ry, RYC, "attn_hi", [128, NCD * TW], BF16, 16544)
    attn_lo = place(ry, RYC, "attn_lo", [128, NCD * TW], BF16, 20640)
    sel16 = place(ry, RYC, "sel16", [16, 1024], F32, 24736)

    rows1 = place(rr, 10272, "rows1", [128, 2568], F32, 0)
    rows2 = place(rr, 10272, "rows2", [128, 2048], F32, 0)
    wstage = place(ws_r, 2048, "wstage", [128, 512], F32, 0)
    wst_i = (A("wst_i", [128, 512], I32) if sim_safe else
             place(ws_r, 2048, "wst_i", [128, 512], I32, 0))

    ones = A("ones", [128, 256], F32)
    ident = A("ident", [128, 128], F32)
    aw = A("aw", [128, 512], F32)
    mbq = A("mbq", [128, 1024], BF16)
    wcols = A("wcols", [128, 128], F32)
    cells = A("cells", [128, 8], F32)
    scol = A("scol", [128, 48], F32)
    rqpair = A("rqpair", [128, 4], F32)
    mr_t = A("mr_t", [128, 2 * NSC], F32)
    csts = A("csts", [128, 4], F32)
    fence = A("zzfence", [128, 4], F32)
    fm = nc.lookup_mloc(fence)
    assert sim_safe or int(fm.addr) + 16 <= 196608, \
        ("SBUF overflow", int(fm.addr))
    nc.const_aps.register(F32, EPS_LN, csts[:, 0:1])
    nc.const_aps.register(F32, EPS_Z, csts[:, 1:2])
    nc.const_aps.register(F32, 1.0 / D, csts[:, 2:3])

    pA = nc.alloc_psum_tensor("pA", [128, 1024], F32)
    pB = nc.alloc_psum_tensor("pB", [128, 1024], F32)
    pC = nc.alloc_psum_tensor("pC", [128, 512], F32)
    pD = nc.alloc_psum_tensor("pD", [128, 512], F32)
    pE = nc.alloc_psum_tensor("pE", [128, 512], F32)
    pF = nc.alloc_psum_tensor("pF", [128, 512], F32)

    s = Sched(paranoid)
    psum_tok = {i: None for i in range(8)}

    def mm_group(banks, mms, extra_waits=()):
        waits = list(extra_waits) + [psum_tok[b] for b in banks]
        def fn(t, mms=mms):
            last = None
            for (o, l, r, kw) in mms:
                last = t.matmul(o, l, r, **kw)
            return last
        return s.emit("tensor", fn, waits=waits)

    def read_psum(tok, banks):
        for b in banks:
            psum_tok[b] = tok

    def ev(eng, fn, waits=(), strict=False):
        return s.emit(eng, fn, waits=waits, strict=strict)

    def dma(eng, out, in_, waits=(), stream=None):
        return s.emit(eng, (lambda e, o=out, i=in_: e.dma_start(out=o, in_=i)),
                      waits=waits, inc=16, dma=True, stream=stream)

    tok_out = []

    def dump_sbuf(name, ap, waits):
        if name in dumps:
            dd = nc.dram_tensor("dmp_" + name, list(ap.shape), ap.dtype,
                                kind="ExternalOutput").ap()
            dump_outs[name] = dd
            tok_out.append(dma("gpsimd", dd, ap, waits=waits))

    t_ones = ev("gpsimd", lambda g: g.memset(ones[:, :], 1.0))
    t_c0 = ev("gpsimd", lambda g: g.memset(csts[:, 0:1], EPS_LN))
    t_c1 = ev("gpsimd", lambda g: g.memset(csts[:, 1:2], EPS_Z))
    t_c2 = ev("gpsimd", lambda g: g.memset(csts[:, 2:3], 1.0 / D))
    t_csts = t_c2
    t_kv0 = ev("gpsimd", lambda g: g.memset(kvacc[:, :], 0.0))
    t_id = dma("gpsimd", ident[:, :], d_ident, stream="s_const")

    tx_l = [dma("sync", x_f[:, c * S:(c + 1) * S],
                d_xT[c * 128:(c + 1) * 128, :], stream="s_x")
            for c in range(NCD)]
    tx = tx_l

    _tb1 = dma("sync", rows2[64:65, 0:D], d_b["bk"], stream="s_b1")
    t_bv = dma("sync", rows2[64:65, D:2 * D], d_b["bv"], stream="s_b1")
    t_bk = t_bv

    last_wc = None
    for t3 in range(4):
        src = d_cw[t3:t3 + 1, :] if t3 < 3 else d_cb[0:1, :]
        tt = dma("sync", tmp32[:, :], src.rearrange("o (j p) -> (o j) p", p=128),
                 waits=[last_wc], stream="s_cw")
        tm = mm_group([7], [(pF[:, 0:32], tmp32[:, :], ident[0:32, 0:32],
                             dict(start=True, stop=True, is_transpose=True))],
                      extra_waits=[tt, t_id])
        last_wc = ev("vector", lambda v, t3=t3: v.tensor_copy(
            wcols[:, t3 * 32:(t3 + 1) * 32], pF[:, 0:32]), waits=[tm],
            strict=True)
        read_psum(last_wc, [7])
    last_wc = ev("vector", lambda v: v.tensor_scalar(
        wcols[:, 96:128], wcols[:, 96:128], 0.5, None, OP.mult),
        waits=[last_wc], strict=True)
    t_wcols = last_wc

    wst_tok = [None]

    def load_split_W(dram_ap, hi_t, lo_t, free_tok):
        last = None
        for c in range(NCD):
            for h2 in range(2):
                lo_c = c * 1024 + h2 * 512
                td = dma("scalar", wstage[:, :],
                         dram_ap[c * 128:(c + 1) * 128,
                                 h2 * 512:(h2 + 1) * 512],
                         waits=[wst_tok[0], free_tok], stream="s_ws")
                th = ev("vector", lambda v, lo_c=lo_c, hi=hi_t: v.tensor_copy(
                    hi[:, lo_c:lo_c + 512], wstage[:, :]),
                    waits=[td, free_tok])
                tl = ev("vector", lambda v, lo_c=lo_c, hi=hi_t, lo=lo_t:
                        v.tensor_tensor(lo[:, lo_c:lo_c + 512],
                                        wstage[:, :],
                                        hi[:, lo_c:lo_c + 512], OP.subtract),
                        waits=[th])
                wst_tok[0] = tl
                last = tl
        return last

    t_wk = load_split_W(d_w["wk"], wA_hi, wA_lo, None)
    t_wv = load_split_W(d_w["wv"], wB_hi, wB_lo, None)

    xt_guard = None
    xt_read = None
    t6a_prev = [None]
    mr_toks = []
    for sub in range(NSC):
        evs = []
        for c in range(NCD):
            pt, bk = (pC, 4) if c % 2 == 0 else (pD, 5)
            tm = mm_group([bk], [(pt[:, 0:128],
                                  x_f[:, c * S + sub * 128:c * S + sub * 128 + 128],
                                  ident[:, :],
                                  dict(start=True, stop=True, is_transpose=True))],
                          extra_waits=[tx[c], t_id])
            te = ev("vector", lambda v, c=c, pt=pt: v.tensor_copy(
                xt_sub[:, c * 128:(c + 1) * 128], pt[:, 0:128]),
                waits=[tm, xt_guard])
            read_psum(te, [bk])
            evs.append(te)
        t1 = ev("vector", lambda v: v.tensor_reduce(
            scol[:, 0:1], xt_sub[:, :], AX.X, OP.add), waits=evs, strict=True)
        t2 = ev("vector", lambda v: v.tensor_scalar(
            scol[:, 1:2], scol[:, 0:1], 1.0 / D, None, OP.mult), waits=[t1],
            strict=True)
        t3 = ev("vector", lambda v: v.tensor_scalar(
            tq[:, :], xt_sub[:, :], scol[:, 1:2], None, OP.subtract), waits=[t2])
        t4 = ev("vector", lambda v: v.tensor_tensor(
            xt_sub[:, :], tq[:, :], tq[:, :], OP.mult), waits=[t3])
        xt_guard = t4
        t5 = ev("vector", lambda v: v.tensor_reduce(
            scol[:, 2:3], xt_sub[:, :], AX.X, OP.add),
            waits=[t4, t6a_prev[0]], strict=True)
        xt_read = t5
        t6a = ev("scalar", lambda sc: sc.activation(
            scol[:, 4:5], scol[:, 2:3], AF.Sqrt, bias=EPS_LN, scale=1.0 / D),
            waits=[t5, t_csts])
        t6a_prev[0] = t6a
        t6 = ev("vector", lambda v: v.reciprocal(scol[:, 3:4], scol[:, 4:5]),
                waits=[t6a], strict=True)
        mr_toks.append(ev("vector", lambda v, sub=sub: v.tensor_copy(
            mr_t[:, 2 * sub:2 * sub + 1], scol[:, 1:2]), waits=[t2]))
        mr_toks.append(ev("vector", lambda v, sub=sub: v.tensor_copy(
            mr_t[:, 2 * sub + 1:2 * sub + 2], scol[:, 3:4]), waits=[t6]))
        if sub == 0:
            dump_sbuf("scol0", scol[:, 0:8], [mr_toks[-1], t6, t5])
            dump_sbuf("xts0", xt_sub[:, :], [t5])
            dump_sbuf("tq0", tq[:, :], [t5, t4])
    tmprow_guard = [None]
    dump_sbuf("mrt", mr_t[:, :], [mr_toks[-1]])

    xn_last_use = [None]
    etmp_guard = [None]
    rowxn_guard = [None]

    def build_xn(a, extra=()):
        row_toks = []
        for which in range(2):
            for jj in range(2):
                col = 2 * (2 * a + jj) + which
                tm = mm_group([6], [(pE[0:1, 0:128],
                                     mr_t[:, col:col + 1], ident[:, :],
                                     dict(start=True, stop=True,
                                          is_transpose=True))],
                              extra_waits=[mr_toks[col], t_id] + list(extra))
                te = ev("vector", lambda v, which=which, jj=jj: v.tensor_copy(
                    rows1[0:1, which * 256 + jj * 128:
                          which * 256 + (jj + 1) * 128],
                    pE[0:1, 0:128]), waits=[tm, rowxn_guard[0]])
                read_psum(te, [6])
                row_toks.append(te)
        tb1 = mm_group([6], [(pE[:, 0:TW], ones[0:1, 0:128],
                              rows1[0:1, 0:TW],
                              dict(start=True, stop=True))],
                       extra_waits=row_toks + [t_ones] + list(extra))
        tb2 = mm_group([5], [(pD[:, 0:TW], ones[0:1, 0:128],
                              rows1[0:1, TW:2 * TW],
                              dict(start=True, stop=True))],
                       extra_waits=row_toks + [t_ones])
        rowxn_guard[0] = tb2
        toks = []
        prev = xn_last_use[0]
        last_t1 = None
        last_t2 = None
        for c in range(NCD):
            t1 = ev("vector", lambda v, c=c, a=a: v.tensor_tensor(
                ttmp[:, :],
                x_f[:, c * S + a * TW:c * S + (a + 1) * TW],
                pE[:, 0:TW], OP.subtract), waits=[tb1, tx[c], prev])
            t2 = ev("vector", lambda v: v.tensor_tensor(
                ttmp[:, :], ttmp[:, :],
                pD[:, 0:TW], OP.mult), waits=[tb2])
            t3 = ev("vector", lambda v, c=c: v.tensor_copy(
                xn_hi[:, c * TW:(c + 1) * TW], ttmp[:, :]),
                waits=[t2])
            t4 = ev("vector", lambda v, c=c: v.tensor_tensor(
                xn_lo[:, c * TW:(c + 1) * TW], ttmp[:, :],
                xn_hi[:, c * TW:(c + 1) * TW], OP.subtract), waits=[t3])
            toks.append(t4)
            last_t1 = t1
            last_t2 = t2
        read_psum(last_t1, [6])
        read_psum(last_t2, [5])
        return toks

    aw_guard = [None]

    def wq_passA(dram, idx, n_pieces, npiece_cols, extra=()):
        rc, cb = idx // npiece_cols, idx % npiece_cols
        tg = None
        for h2 in range(2):
            td = dma("sync", wstage[:, :],
                     dram[rc * 128:(rc + 1) * 128,
                          cb * 1024 + h2 * 512:cb * 1024 + (h2 + 1) * 512],
                     waits=[wst_tok[0]] + list(extra), stream="s_ws")
            ta = ev("vector", lambda v: v.scalar_tensor_tensor(
                aw[:, :], wstage[:, :], -1.0, wstage[:, :], OP.mult, OP.max),
                waits=[td, aw_guard[0]])
            wst_tok[0] = ta
            tg = mm_group([7], [(pF[0:1, 0:512], ones[:, 0:1], aw[:, :],
                                 dict(start=(idx == 0 and h2 == 0),
                                      stop=(idx == n_pieces - 1 and h2 == 1)))],
                          extra_waits=[ta, t_ones])
            aw_guard[0] = tg
        return tg

    def wq_finalize(nelem, ci_scale, ci_recip, last_tok):
        te = ev("vector", lambda v: v.tensor_copy(tmprow[0:1, 0:128],
                                                  pF[0:1, 0:128]),
                waits=[last_tok] + tmprow_guard)
        acc = te
        for kpart in range(1, 4):
            tc = ev("vector", lambda v, kpart=kpart: v.tensor_copy(
                tmp32[0:1, 0:128], pF[0:1, kpart * 128:(kpart + 1) * 128]),
                waits=[acc])
            acc = ev("vector", lambda v: v.tensor_tensor(
                tmprow[0:1, 0:128], tmprow[0:1, 0:128], tmp32[0:1, 0:128],
                OP.add), waits=[tc])
        read_psum(acc, [7])
        tr = ev("vector", lambda v, ci=ci_scale: v.tensor_reduce(
            rows1[0:1, 2562 + ci:2563 + ci], tmprow[0:1, 0:128], AX.X, OP.add),
            waits=[acc], strict=True)
        tm2 = ev("vector", lambda v, ci=ci_scale, nelem=nelem: v.tensor_scalar(
            rows1[0:1, 2562 + ci:2563 + ci], rows1[0:1, 2562 + ci:2563 + ci],
            1.0 / nelem, QCLIP, OP.mult, OP.max), waits=[tr])
        tb = mm_group([7], [(pF[:, 0:1], ones[0:1, 0:128],
                             rows1[0:1, 2562 + ci_scale:2563 + ci_scale],
                             dict(start=True, stop=True))],
                      extra_waits=[tm2, t_ones])
        tc1 = ev("vector", lambda v, ci=ci_scale: v.tensor_scalar(
            cells[:, ci:ci + 1], pF[:, 0:1], 1.0 / 127.0, None, OP.mult),
            waits=[tb], strict=True)
        tc2a = ev("vector", lambda v: v.tensor_copy(cells[:, 7:8],
                                                    pF[:, 0:1]), waits=[tb])
        tc2 = ev("vector", lambda v, ci=ci_recip: v.reciprocal(
            cells[:, ci:ci + 1], cells[:, 7:8]), waits=[tc2a], strict=True)
        read_psum(tc2, [7])
        tmprow_guard[:] = [tr]
        return (tc1, tc2)

    mi32_guard = [None]
    mbq_guard = [None, None]

    def wq_passB(dram, dspill, idx, npiece_cols, is_w1, ci_recip, extra=()):
        rc, cb = idx // npiece_cols, idx % npiece_cols
        mb = mbq if (is_w1 or idx % 2 == 0) else mbq2
        gslot = 0 if (is_w1 or idx % 2 == 0) else 1
        toks = []
        tcl_l = []
        for half in range(2):
            td = dma("sync", wstage[:, :],
                     dram[rc * 128:(rc + 1) * 128,
                          cb * 1024 + half * 512:cb * 1024 + (half + 1) * 512],
                     waits=[wst_tok[0]] + list(extra), stream="s_ws")
            ti = ev("vector", lambda v, ci=ci_recip: v.tensor_scalar(
                wst_i[:, :], wstage[:, :], cells[:, ci:ci + 1], None, OP.mult),
                waits=[td])
            tc = ev("vector", lambda v, half=half, mb=mb: v.tensor_copy(
                mb[:, half * 512:(half + 1) * 512], wst_i[:, :]),
                waits=[ti] + ([mbq_guard[gslot]] if mbq_guard[gslot] else []))
            wst_tok[0] = tc
            tcl = ev("vector", lambda v, half=half, mb=mb: v.tensor_scalar(
                mb[:, half * 512:(half + 1) * 512],
                mb[:, half * 512:(half + 1) * 512],
                1.0, -1.0, OP.min, OP.max), waits=[tc])
            tcl_l.append(tcl)
            if is_w1:
                hb = 2 * cb + half
                toks.append(dma(
                    "gpsimd",
                    dspill[(hb * 8 + rc) * 128:(hb * 8 + rc + 1) * 128, :],
                    mb[:, half * 512:(half + 1) * 512], waits=[tcl],
                    stream="s_m1sp"))
        if not is_w1:
            for ob in range(8):
                toks.append(dma(
                    "scalar",
                    dspill[ob * 128:(ob + 1) * 128,
                           rc * 128:(rc + 1) * 128],
                    mb[:, ob * 128:(ob + 1) * 128], waits=tcl_l,
                    stream="s_m2sp"))
        mbq_guard[gslot] = toks[-1]
        return toks

    kv_read_tok = None
    kv_add_tok = None
    w1a_i = [0]
    w1a_last = [None]
    p1_last_mm = None
    for a in range(NT):
        xn_toks = build_xn(a)
        for j in range(2):
            sc = a * 2 + j
            kgroups = []
            vgroups = []
            for (proj, whi, wlo, wtok, btok) in (
                    ("k", wA_hi, wA_lo, t_wk, t_bk),
                    ("v", wB_hi, wB_lo, t_wv, t_bv)):
                pt = pA if proj == "k" else pB
                bks = (0, 1) if proj == "k" else (2, 3)
                for n in range(2):
                    mms = []
                    first = True
                    for (xs, ws) in ((xn_hi, whi), (xn_lo, whi), (xn_hi, wlo)):
                        for c in range(NCD):
                            mms.append((pt[:, n * 512:(n + 1) * 512],
                                        xs[:, c * TW + j * 128:
                                           c * TW + j * 128 + 128],
                                        ws[:, c * D + n * 512:
                                           c * D + (n + 1) * 512],
                                        dict(start=first, stop=False)))
                            first = False
                    bones = ones[64:65, 0:128]
                    brow = (rows2[64:65, n * 512:(n + 1) * 512] if proj == "k"
                            else rows2[64:65, D + n * 512:D + (n + 1) * 512])
                    mms.append((pt[:, n * 512:(n + 1) * 512], bones, brow,
                                dict(start=False, stop=True,
                                     tile_position=(64, 0))))
                    tg = mm_group([bks[n]], mms,
                                  extra_waits=xn_toks + [wtok, btok, t_ones,
                                                         kv_read_tok])
                    (kgroups if proj == "k" else vgroups).append(tg)
            tmin = ev("vector", lambda v: v.tensor_scalar(
                etmp[:, 0:1024], pA[:, 0:1024], 0.0, None, OP.min),
                waits=kgroups + [etmp_guard[0]])
            texp = ev("scalar", lambda sc_: sc_.activation(
                kp[:, :], etmp[:, 0:1024], AF.Exp), waits=[tmin])
            etmp_guard[0] = texp
            tk = ev("vector", lambda v: v.scalar_tensor_tensor(
                kp[:, :], pA[:, 0:1024], 0.0, kp[:, :],
                OP.max, OP.add), waits=[texp])
            read_psum(tk, [0, 1])
            tv = ev("vector", lambda v: v.tensor_copy(vp[:, :], pB[:, 0:1024]),
                    waits=vgroups)
            read_psum(tv, [2, 3])
            kvg = []
            for half in range(2):
                pt, bk = (pC, 4) if half == 0 else (pD, 5)
                mms = [(pt[:, 500:501], ones[:, 0:128], ones[:, 0:1],
                        dict(start=True, stop=False))]
                for pp in range(4):
                    p = half * 4 + pp
                    for parity in range(2):
                        h = 2 * p + parity
                        base = parity * 64
                        kw = dict(start=False, stop=False)
                        if parity:
                            kw["tile_position"] = (0, 64)
                        mms.append((pt[base:base + 64, pp * 65:pp * 65 + 64],
                                    kp[:, h * 64:(h + 1) * 64],
                                    vp[:, h * 64:(h + 1) * 64], kw))
                        kw2 = dict(start=False, stop=False)
                        if parity:
                            kw2["tile_position"] = (0, 64)
                        mms.append((pt[base:base + 64,
                                       pp * 65 + 64:pp * 65 + 65],
                                    kp[:, h * 64:(h + 1) * 64],
                                    ones[:, 0:1], kw2))
                mms.append((pt[:, 500:501], ones[:, 0:128], ones[:, 0:1],
                            dict(start=False, stop=True)))
                kvg.append(mm_group([bk], mms, extra_waits=[tk, tv, t_ones]))
            ta1 = ev("vector", lambda v: v.tensor_tensor(
                kvacc[:, 0:260], kvacc[:, 0:260], pC[:, 0:260], OP.add),
                waits=[kvg[0], t_kv0, kv_add_tok])
            read_psum(ta1, [4])
            ta2 = ev("vector", lambda v: v.tensor_tensor(
                kvacc[:, 260:520], kvacc[:, 260:520], pD[:, 0:260], OP.add),
                waits=[kvg[1], t_kv0, ta1])
            read_psum(ta2, [5])
            kv_add_tok = ta2
            kv_read_tok = kvg[1]
            p1_last_mm = kvg[1]
            xn_last_use[0] = kvg[1]
            for _ in range(2):
                if w1a_i[0] < 32:
                    w1a_last[0] = wq_passA(d_w1t, w1a_i[0], 32, 4)
                    w1a_i[0] += 1
    while w1a_i[0] < 32:
        w1a_last[0] = wq_passA(d_w1t, w1a_i[0], 32, 4)
        w1a_i[0] += 1
    dump_sbuf("kvacc", kvacc[:, :], [kv_add_tok])

    c_w1 = wq_finalize(D * DH, 0, 1, w1a_last[0])

    t_wq = load_split_W(d_w["wq"], wA_hi, wA_lo, p1_last_mm)
    t_wo = load_split_W(d_w["wo"], wB_hi, wB_lo, p1_last_mm)
    _tb2 = dma("sync", rows2[64:65, 0:D], d_b["bq"], waits=[p1_last_mm],
               stream="s_b2")
    t_bo = dma("sync", rows2[64:65, D:2 * D], d_b["bo"], waits=[p1_last_mm],
               stream="s_b2")
    t_bq = t_bo
    t_sel = dma("gpsimd", sel16[:, :], d_sel, waits=[p1_last_mm, tk],
                stream="s_sel")

    w1b_i = [0]
    w2a_i = [0]
    w2a_last = [None]
    m1q_spills = []
    attn_guard = None
    xsp_toks = {}
    rz_guard = [None]
    rzT_guard = [c_w1[1]]
    for a in range(NT):
        xn_toks = build_xn(a, extra=[t_wq])
        q_toks = []
        for o in range(NCD):
            half = o % 2
            pq = pA[:, half * 512:half * 512 + TW]
            bk = half
            mms = []
            first = True
            for (xs, ws) in ((xn_hi, wA_hi), (xn_lo, wA_hi), (xn_hi, wA_lo)):
                for c in range(NCD):
                    mms.append((pq, ws[:, c * D + o * 128:c * D + (o + 1) * 128],
                                xs[:, c * TW:(c + 1) * TW],
                                dict(start=first, stop=False)))
                    first = False
            mms.append((pq, rows2[64:65, o * 128:(o + 1) * 128],
                        ones[64:65, 0:TW], dict(start=False, stop=True,
                                                tile_position=(64, 0))))
            tg = mm_group([bk], mms, extra_waits=xn_toks + [t_wq, t_bq, t_ones])
            tmin = ev("vector", lambda v, pq=pq: v.tensor_scalar(
                etmp[:, 0:TW], pq, 0.0, None, OP.min),
                waits=[tg, etmp_guard[0]])
            texp = ev("scalar", lambda sc_: sc_.activation(
                etmp[:, TW:2 * TW], etmp[:, 0:TW], AF.Exp), waits=[tmin])
            etmp_guard[0] = texp
            tq2 = ev("vector", lambda v, o=o, pq=pq: v.scalar_tensor_tensor(
                qsb[:, o * TW:(o + 1) * TW], pq, 0.0, etmp[:, TW:2 * TW],
                OP.max, OP.add), waits=[texp])
            read_psum(tq2, [bk])
            q_toks.append(tq2)
            xn_last_use[0] = tg
        rz_toks = []
        for j in range(2):
            mms = []
            for h in range(16):
                p = h // 2
                base = (h % 2) * 64
                kwz = dict(start=(h == 0), stop=(h == 15))
                if base:
                    kwz["tile_position"] = (64, 0)
                mms.append((pB[:, j * 16 + h:j * 16 + h + 1],
                            qsb[base:base + 64,
                                p * TW + j * 128:p * TW + j * 128 + 128],
                            kvacc[base:base + 64, p * 65 + 64:p * 65 + 65],
                            kwz))
            tzg = mm_group([2], mms,
                           extra_waits=q_toks + [kv_add_tok] + rz_toks)
            tze = ev("vector", lambda v, j=j: v.tensor_scalar(
                scol[:, 0:16], pB[:, j * 16:(j + 1) * 16], EPS_Z, None,
                OP.add), waits=[tzg, rz_guard[0]])
            trz = ev("vector", lambda v, j=j: v.reciprocal(
                rz_t[:, j * 16:(j + 1) * 16], scol[:, 0:16]), waits=[tze],
                strict=True)
            read_psum(trz, [2])
            rz_toks.append(trz)
        rzT_toks = []
        for j in range(2):
            tm = mm_group([3], [(pB[0:16, 512:640], rz_t[:, j * 16:(j + 1) * 16],
                                 ident[:, :],
                                 dict(start=True, stop=True,
                                      is_transpose=True))],
                          extra_waits=[rz_toks[j], t_id])
            te = ev("vector", lambda v, j=j: v.tensor_copy(
                rzT[0:16, j * 128:(j + 1) * 128], pB[0:16, 512:640]),
                waits=[tm, rzT_guard[0]])
            read_psum(te, [3])
            rzT_toks.append(te)
        rz_guard[0] = tm
        qpp_toks = []
        for c in range(NCD):
            tm = mm_group([3], [(pB[:, 512:512 + TW],
                                 sel16[0:16, c * 128:(c + 1) * 128],
                                 rzT[0:16, 0:TW], dict(start=True, stop=True))],
                          extra_waits=rzT_toks + [t_sel])
            tqq = ev("vector", lambda v, c=c: v.tensor_tensor(
                qsb[:, c * TW:(c + 1) * TW], qsb[:, c * TW:(c + 1) * TW],
                pB[:, 512:512 + TW], OP.mult), waits=[tm])
            read_psum(tqq, [3])
            qpp_toks.append(tqq)
        rzT_guard[0] = tm
        pair_toks = []
        for p in range(8):
            pt, bk = (pC, 4) if p % 2 == 0 else (pD, 5)
            mms = [(pt[:, 400:401], ones[:, 0:128], ones[:, 0:1],
                    dict(start=True, stop=False))]
            for parity in range(2):
                h = 2 * p + parity
                base = parity * 64
                kw = dict(start=False, stop=False)
                if parity:
                    kw["tile_position"] = (64, 64)
                mms.append((pt[base:base + 64, 0:TW],
                            kvacc[base:base + 64, p * 65:p * 65 + 64],
                            qsb[base:base + 64, p * TW:(p + 1) * TW], kw))
            mms.append((pt[:, 400:401], ones[:, 0:128], ones[:, 0:1],
                        dict(start=False, stop=True)))
            tg = mm_group([bk], mms,
                          extra_waits=[qpp_toks[p], kv_add_tok, attn_guard])
            th = ev("vector", lambda v, p=p, pt=pt: v.tensor_copy(
                attn_hi[:, p * TW:(p + 1) * TW], pt[:, 0:TW]),
                waits=[tg])
            tl = ev("vector", lambda v, p=p, pt=pt: v.tensor_tensor(
                attn_lo[:, p * TW:(p + 1) * TW], pt[:, 0:TW],
                attn_hi[:, p * TW:(p + 1) * TW], OP.subtract), waits=[th])
            read_psum(tl, [bk])
            pair_toks.append(tl)
        for o in range(NCD):
            mms = []
            first = True
            for (xs, ws) in ((attn_hi, wB_hi), (attn_lo, wB_hi),
                             (attn_hi, wB_lo)):
                for c in range(NCD):
                    mms.append((pE[:, 0:TW],
                                ws[:, c * D + o * 128:c * D + (o + 1) * 128],
                                xs[:, c * TW:(c + 1) * TW],
                                dict(start=first, stop=False)))
                    first = False
            mms.append((pE[:, 0:TW],
                        rows2[64:65, D + o * 128:D + (o + 1) * 128],
                        ones[64:65, 0:TW], dict(start=False, stop=True,
                                                tile_position=(64, 0))))
            tg = mm_group([6], mms,
                          extra_waits=pair_toks + [t_wo, t_bo, t_ones])
            tex = ev("vector", lambda v, o=o, a=a: v.tensor_tensor(
                x_f[:, o * S + a * TW:o * S + (a + 1) * TW],
                x_f[:, o * S + a * TW:o * S + (a + 1) * TW],
                pE[:, 0:TW], OP.add), waits=[tg])
            read_psum(tex, [6])
            attn_guard = tg
            if o == NCD - 1:
                wo_last = tex
            if o == 0:
                xsp_toks[a] = []
            xsp_toks[a].append(dma(
                "gpsimd", d_xsp[o * 128:(o + 1) * 128, a * TW:(a + 1) * TW],
                x_f[:, o * S + a * TW:o * S + (a + 1) * TW], waits=[tex],
                stream="s_xsp"))
        xn_last_use[0] = attn_guard
        for _ in range(4):
            if w1b_i[0] < 32:
                m1q_spills += wq_passB(d_w1t, d_m1q, w1b_i[0], 4, True, 1,
                                       extra=list(c_w1))
                w1b_i[0] += 1
        for _ in range(4):
            if w2a_i[0] < 32:
                w2a_last[0] = wq_passA(d_w2t, w2a_i[0], 32, 1,
                                       extra=[c_w1[0]])
                w2a_i[0] += 1
    while w1b_i[0] < 32:
        m1q_spills += wq_passB(d_w1t, d_m1q, w1b_i[0], 4, True, 1,
                               extra=list(c_w1))
        w1b_i[0] += 1
    while w2a_i[0] < 32:
        w2a_last[0] = wq_passA(d_w2t, w2a_i[0], 32, 1, extra=[c_w1[0]])
        w2a_i[0] += 1
    dump_sbuf("xattn", x_f[:, :], [wo_last])
    dump_sbuf("qpp", qsb[:, :], qpp_toks)
    dump_sbuf("attnhi", attn_hi[:, :], pair_toks)
    dump_sbuf("rzt", rz_t[:, :], rz_toks)

    tmprow_guard.append(rzT_guard[0])
    c_w2 = wq_finalize(DH * D, 2, 3, w2a_last[0])

    m2q_spills = []

    xsp_all = [xsp_toks[NT - 1][-1]]
    t_rq0 = ev("gpsimd", lambda g: g.memset(rows1[0:1, 0:1], 0.0),
               waits=xsp_all)
    t_rq1 = ev("gpsimd", lambda g: g.memset(rows1[0:1, S + 1:S + 2], 0.0),
               waits=xsp_all)

    def dump_xsp(name, waits):
        if name not in dumps:
            return
        dd = nc.dram_tensor("dmp_" + name, [128, 2048], F32,
                            kind="ExternalOutput").ap()
        dump_outs[name] = dd
        prev = list(waits) + [wst_tok[0]]
        for half in range(4):
            tl = dma("gpsimd", wstage[:, :],
                     d_xsp[0:128, half * 512:(half + 1) * 512],
                     waits=prev, stream="s_xdbg")
            ts_ = dma("gpsimd", dd[:, half * 512:(half + 1) * 512],
                      wstage[:, :], waits=[tl], stream="s_xdbg")
            prev = [ts_]
        tok_out.append(prev[0])
        wst_tok[0] = prev[0]

    dump_xsp("xspearly", xsp_all + m2q_spills[-1:])
    xf0_tok = []
    if "xf0" in dumps:
        dd0 = nc.dram_tensor("dmp_xf0", [128, 2048], F32,
                             kind="ExternalOutput").ap()
        dump_outs["xf0"] = dd0
        xf0_tok.append(dma("gpsimd", dd0, x_f[:, 0:2048],
                           waits=[wo_last] + xsp_all, stream="s_xdbg"))
        tok_out.append(xf0_tok[0])
    n1_tok = {}
    n1_slot_use = {}
    rq_row_toks = {}
    last_q_stage = [None]
    xtq_guard = [None]
    tq_guard = [None]
    scol_guard = {0: None, 1: None}
    t6a_guard = {0: None, 1: None}
    rqp_guard = [None]

    def quant_stage(i):
        sl = i % 3
        xr_l = [dma("sync", xqread[:, c * TW:(c + 1) * TW],
                    d_xsp[c * 128:(c + 1) * 128, i * TW:(i + 1) * TW],
                    waits=xsp_all + [last_q_stage[0]] + xf0_tok,
                    stream="s_xq")
                for c in range(NCD)]
        xr = [xr_l[-1]] * NCD
        sub_toks = []
        rqp_toks = []
        for j in range(2):
            evs = []
            for c in range(NCD):
                pt, bk = (pC, 4) if c % 2 == 0 else (pD, 5)
                tm = mm_group([bk], [(pt[:, 0:128],
                                      xqread[:, c * TW + j * 128:
                                             c * TW + j * 128 + 128],
                                      ident[:, :],
                                      dict(start=True, stop=True,
                                           is_transpose=True))],
                              extra_waits=[xr[c], t_id])
                te = ev("vector", lambda v, c=c, pt=pt: v.tensor_copy(
                    xt_sub[:, c * 128:(c + 1) * 128], pt[:, 0:128]),
                    waits=[tm, xtq_guard[0], last_q_stage[0]])
                read_psum(te, [bk])
                evs.append(te)
            sb = j * 16
            t1 = ev("vector", lambda v, sb=sb: v.tensor_reduce(
                scol[:, sb:sb + 1], xt_sub[:, :], AX.X, OP.add), waits=evs,
                strict=True)
            t2 = ev("vector", lambda v, sb=sb: v.tensor_scalar(
                scol[:, sb + 1:sb + 2], scol[:, sb:sb + 1], 1.0 / D, None,
                OP.mult), waits=[t1], strict=True)
            t3 = ev("vector", lambda v, sb=sb: v.tensor_scalar(
                tq[:, :], xt_sub[:, :], scol[:, sb + 1:sb + 2], None,
                OP.subtract), waits=[t2, tq_guard[0]])
            t4 = ev("vector", lambda v: v.tensor_tensor(
                xt_sub[:, :], tq[:, :], tq[:, :], OP.mult),
                waits=[t3, scol_guard[j]])
            xtq_guard[0] = t4
            t5 = ev("vector", lambda v, sb=sb: v.tensor_reduce(
                scol[:, sb + 2:sb + 3], xt_sub[:, :], AX.X, OP.add),
                waits=[t4, t6a_guard[j]], strict=True)
            t6a = ev("scalar", lambda sc_, sb=sb: sc_.activation(
                scol[:, sb + 10:sb + 11], scol[:, sb + 2:sb + 3], AF.Sqrt,
                bias=EPS_LN, scale=1.0 / D), waits=[t5, scol_guard[j], t_csts])
            t6a_guard[j] = t6a
            t6 = ev("vector", lambda v, sb=sb: v.reciprocal(
                scol[:, sb + 3:sb + 4], scol[:, sb + 10:sb + 11]),
                waits=[t6a], strict=True)
            t7 = ev("vector", lambda v, sb=sb: v.tensor_reduce(
                scol[:, sb + 4:sb + 5], tq[:, :], AX.X, OP.max,
                apply_absolute_value=True), waits=[t3], strict=True)
            t8 = ev("vector", lambda v, sb=sb: v.tensor_tensor(
                scol[:, sb + 5:sb + 6], scol[:, sb + 4:sb + 5],
                scol[:, sb + 3:sb + 4], OP.mult), waits=[t6, t7])
            t9 = ev("vector", lambda v, sb=sb: v.tensor_scalar(
                scol[:, sb + 6:sb + 7], scol[:, sb + 5:sb + 6], QCLIP, None,
                OP.max), waits=[t8])
            t10 = ev("vector", lambda v, sb=sb: v.reciprocal(
                scol[:, sb + 7:sb + 8], scol[:, sb + 6:sb + 7]), waits=[t9],
                strict=True)
            t11 = ev("vector", lambda v, sb=sb: v.tensor_scalar(
                scol[:, sb + 8:sb + 9], scol[:, sb + 7:sb + 8], 127.0, None,
                OP.mult), waits=[t10])
            t12 = ev("vector", lambda v, sb=sb: v.tensor_tensor(
                scol[:, sb + 9:sb + 10], scol[:, sb + 3:sb + 4],
                scol[:, sb + 8:sb + 9], OP.mult), waits=[t11], strict=True)
            t13 = ev("vector", lambda v, j=j, sb=sb: v.tensor_tensor(
                rqpair[:, j:j + 1], scol[:, sb + 6:sb + 7], cells[:, 0:1],
                OP.mult), waits=[t9, c_w1[0], rqp_guard[0]])
            rqp_toks.append(t13)
            ti = ev("vector", lambda v, sb=sb: v.tensor_scalar(
                n1t_i[:, :], tq[:, :], scol[:, sb + 9:sb + 10], None, OP.mult),
                waits=[t12, mi32_guard[0]])
            tq_guard[0] = ti
            scol_guard[j] = ti
            tb = ev("vector", lambda v: v.tensor_copy(n1t_f[:, :],
                                                      n1t_i[:, :]), waits=[ti])
            for c in range(NCD):
                pt, bk = (pC, 4) if c % 2 == 0 else (pD, 5)
                tm = mm_group([bk], [(pt[:, 0:128],
                                      n1t_f[:, c * 128:(c + 1) * 128],
                                      ident[:, :],
                                      dict(start=True, stop=True,
                                           is_transpose=True))],
                              extra_waits=[tb, t_id])
                te = ev("vector", lambda v, c=c, j=j, sl=sl, pt=pt:
                        v.tensor_copy(
                            n1ring[sl][:, c * 258 + 1 + j * 128:
                                       c * 258 + 1 + (j + 1) * 128],
                            pt[:, 0:128]),
                        waits=[tm, n1_slot_use.get(sl)])
                read_psum(te, [bk])
                sub_toks.append(te)
            mi32_guard[0] = sub_toks[-1]
        rq_row_toks[i] = []
        for j in range(2):
            tm = mm_group([7], [(pF[0:1, 0:128], rqpair[:, j:j + 1],
                                 ident[:, :],
                                 dict(start=True, stop=True,
                                      is_transpose=True))],
                          extra_waits=[rqp_toks[j], t_id])
            rqp_guard[0] = tm
            te = ev("vector", lambda v, i=i, j=j: v.tensor_copy(
                rows1[0:1, i * TW + j * 128 + 1:i * TW + (j + 1) * 128 + 1],
                pF[0:1, 0:128]), waits=[tm])
            read_psum(te, [7])
            rq_row_toks[i].append(te)
        if "xq2" in dumps and i == 2:
            dump_sbuf("xq2", xqread[:, :], sub_toks)
            dump_sbuf("scol2", scol[:, 0:32], sub_toks)
            last_q_stage[0] = tok_out[-1]
        n1_tok[i] = sub_toks
        if last_q_stage[0] is None or i != 2 or "xq2" not in dumps:
            last_q_stage[0] = sub_toks[-1]

    def stitch(i):
        sl = i % 3
        nr = n1ring[sl].ap().rearrange("p (c w) -> p c w", w=258)
        toks = []
        if i == 0:
            toks.append(ev("vector", lambda v, nr=nr: v.memset(nr[:, :, 0:1],
                                                               0.0),
                           waits=n1_tok[0]))
        else:
            nrp = n1ring[(i - 1) % 3].ap().rearrange("p (c w) -> p c w", w=258)
            toks.append(ev("vector", lambda v, nr=nr, nrp=nrp: v.tensor_copy(
                nr[:, :, 0:1], nrp[:, :, 256:257]),
                waits=n1_tok[i] + n1_tok[i - 1]))
        if i == NT - 1:
            toks.append(ev("vector", lambda v, nr=nr: v.memset(
                nr[:, :, 257:258], 0.0), waits=n1_tok[i]))
        else:
            nrn = n1ring[(i + 1) % 3].ap().rearrange("p (c w) -> p c w", w=258)
            toks.append(ev("vector", lambda v, nr=nr, nrn=nrn: v.tensor_copy(
                nr[:, :, 257:258], nrn[:, :, 1:2]),
                waits=n1_tok[i] + n1_tok[i + 1]))
        return toks

    m1r_tok = [None, None]
    m2r_tok = [None, None]
    hin_tok = [None, None]
    gb_tok = {0: None, 1: None}
    tb_guard = {0: None, 1: None}
    hs_guard = [None]
    M2_guard = [None]
    n2_guard = [None]
    ffA_guard = [None]
    ffB_guard = [None]
    ot_tok = [None, None]
    xfr_guard = [None]
    rowff_guard = [None]

    def ffn_tile(i, st_toks):
        sl = i % 3
        pend = []
        tb_guard[0] = ffA_guard[0]
        if tb_guard[1] is None:
            tb_guard[1] = c_w2[1]
        extra_rq = rq_row_toks[i] + (rq_row_toks.get(i + 1, []) or [t_rq1])
        tbq = mm_group([6], [(pE[:, 0:258], ones[0:1, 0:128],
                              rows1[0:1, i * TW:i * TW + 258],
                              dict(start=True, stop=True))],
                       extra_waits=extra_rq + [t_rq0, t_rq1, t_ones])
        trq = ev("vector", lambda v: v.tensor_copy(rqb[:, :], pE[:, 0:258]),
                 waits=[tbq])
        read_psum(trq, [6])
        xfr_l = [dma("gpsimd", xfread[:, c * TW:(c + 1) * TW],
                     d_xsp[c * 128:(c + 1) * 128, i * TW:(i + 1) * TW],
                     waits=xsp_all + [xfr_guard[0]], stream="s_xf")
                 for c in range(NCD)]
        xfr = [xfr_l[-1]] * NCD
        am_tok = M2_guard[0]
        silu_last = None
        for hb in range(8):
            msl = hb % 2
            mdma_l = [dma("sync", m1r[msl][:, k * 512:(k + 1) * 512],
                          d_m1q[(hb * 8 + k) * 128:(hb * 8 + k + 1) * 128, :],
                          waits=[m1r_tok[msl]] +
                          (m1q_spills[-1:] + xsp_all
                           if (i == 0 and hb < 2) else []),
                          stream=f"s_m1r{msl}")
                      for k in range(NCD)]
            mdma = [mdma_l[-1]]
            for cc in range(4):
                ch = hb * 4 + cc
                hsl = ch % 2
                gb = gbuf if ch % 2 == 0 else gbuf2
                tb = fftmp[:, 256:512] if ch % 2 == 0 else tanh2[:, :]
                pt = pA[:, hsl * 512:hsl * 512 + 258]
                mms = [(pt, m1r[msl][:, k * 512 + cc * 128:
                                     k * 512 + (cc + 1) * 128],
                        n1ring[sl][:, k * 258:(k + 1) * 258],
                        dict(start=(k == 0), stop=(k == NCD - 1)))
                       for k in range(NCD)]
                tg = mm_group([hsl], mms,
                              extra_waits=mdma + st_toks + [hin_tok[hsl]])
                m1r_tok[msl] = tg
                n1_slot_use[sl] = tg
                th = ev("vector", lambda v, hsl=hsl, pt=pt: v.tensor_tensor(
                    h_in[hsl][:, :], pt, rqb[:, :], OP.mult),
                    waits=[tg, trq])
                read_psum(th, [hsl])
                tc1 = ev("vector", lambda g, hsl=hsl, ch=ch, gb=gb:
                         g.tensor_scalar(
                             gb[:, :], h_in[hsl][:, 1:257],
                             wcols[:, 32 + ch:33 + ch], None, OP.mult),
                         waits=[th, t_wcols, gb_tok[ch % 2]])
                tc2 = ev("vector", lambda g, hsl=hsl, ch=ch, gb=gb:
                         g.scalar_tensor_tensor(
                             gb[:, :], h_in[hsl][:, 0:256],
                             wcols[:, ch:ch + 1], gb[:, :], OP.mult, OP.add),
                         waits=[tc1])
                tc3 = ev("vector", lambda g, hsl=hsl, ch=ch, gb=gb:
                         g.scalar_tensor_tensor(
                             gb[:, :], h_in[hsl][:, 2:258],
                             wcols[:, 64 + ch:65 + ch], gb[:, :],
                             OP.mult, OP.add), waits=[tc2])
                hin_tok[hsl] = tc3
                tth = ev("scalar", lambda sc_, ch=ch, gb=gb, tb=tb:
                         sc_.activation(
                             tb, gb[:, :], AF.Tanh, scale=0.5,
                             bias=wcols[:, 96 + ch:97 + ch]),
                         waits=[tc3, tb_guard[ch % 2]])
                ta2 = ev("vector", lambda v, ch=ch, gb=gb: v.tensor_scalar(
                    hs[:, ch * TW:(ch + 1) * TW], gb[:, :], 0.5,
                    wcols[:, 96 + ch:97 + ch], OP.mult, OP.add),
                    waits=[tc3, hs_guard[0]])
                pend.append((ch, gb, tb, tth, ta2))
                if len(pend) >= 2 or ch == NCH - 1:
                    todo = pend[:] if ch == NCH - 1 else [pend.pop(0)]
                    for (ch2, gb2, tb2, tth2, ta22) in todo:
                        tsil = ev("vector",
                                  lambda v, ch2=ch2, tb2=tb2:
                                  v.scalar_tensor_tensor(
                                      hs[:, ch2 * TW:(ch2 + 1) * TW], tb2,
                                      1.0, hs[:, ch2 * TW:(ch2 + 1) * TW],
                                      OP.add, OP.mult),
                                  waits=[tth2, ta22])
                        tb_guard[ch2 % 2] = tsil
                        if ch2 % 2 == 0:
                            ffA_guard[0] = tsil
                        silu_last = tsil
                        tabs = ev("vector",
                                  lambda g, ch2=ch2, gb2=gb2:
                                  g.scalar_tensor_tensor(
                                      gb2[:, :],
                                      hs[:, ch2 * TW:(ch2 + 1) * TW], -1.0,
                                      hs[:, ch2 * TW:(ch2 + 1) * TW],
                                      OP.mult, OP.max),
                                  waits=[tsil])
                        gb_tok[ch2 % 2] = tabs
                        if ch2 == 0:
                            am_tok = ev("vector", lambda v, gb2=gb2:
                                        v.tensor_copy(M2[:, :], gb2[:, :]),
                                        waits=[tabs, M2_guard[0]])
                        else:
                            am_tok = ev("vector", lambda v, gb2=gb2:
                                        v.tensor_tensor(
                                            M2[:, :], M2[:, :], gb2[:, :],
                                            OP.max), waits=[tabs, am_tok])
                    if ch == NCH - 1:
                        pend.clear()
        red_toks = []
        for j in range(2):
            tm = mm_group([7], [(pF[:, 0:128], M2[:, j * 128:(j + 1) * 128],
                                 ident[:, :],
                                 dict(start=True, stop=True,
                                      is_transpose=True))],
                          extra_waits=[am_tok, t_id, rqp_guard[0]])
            tr = ev("vector", lambda v, j=j: v.tensor_reduce(
                scol[:, 32 + j:33 + j], pF[:, 0:128], AX.X, OP.max),
                waits=[tm], strict=True)
            read_psum(tr, [7])
            red_toks.append(tr)
        M2_guard[0] = tm
        t9 = ev("vector", lambda v: v.tensor_scalar(
            scol[:, 34:36], scol[:, 32:34], QCLIP, None, OP.max),
            waits=red_toks)
        t10 = ev("vector", lambda v: v.reciprocal(
            scol[:, 36:38], scol[:, 34:36]), waits=[t9], strict=True)
        t11 = ev("vector", lambda v: v.tensor_scalar(
            scol[:, 38:40], scol[:, 36:38], 127.0, None, OP.mult),
            waits=[t10])
        t12 = ev("vector", lambda v: v.tensor_tensor(
            scol[:, 40:41], scol[:, 34:35], cells[:, 2:3], OP.mult),
            waits=[t9, c_w2[0]])
        t12b = ev("vector", lambda v: v.tensor_tensor(
            scol[:, 41:42], scol[:, 35:36], cells[:, 2:3], OP.mult),
            waits=[t12])
        tp1 = ev("vector", lambda v: v.tensor_copy(rqpair[:, 0:2],
                                                   scol[:, 38:40]),
                 waits=[t11, rqp_guard[0]])
        tp2 = ev("vector", lambda v: v.tensor_copy(rqpair[:, 2:4],
                                                   scol[:, 40:42]),
                 waits=[t12b, tp1])
        rdma = []
        for kk in range(4):
            tm = mm_group([7], [(pF[0:1, 0:128], rqpair[:, kk:kk + 1],
                                 ident[:, :],
                                 dict(start=True, stop=True,
                                      is_transpose=True))],
                          extra_waits=[tp2, t_id])
            rqp_guard[0] = tm
            te = ev("vector", lambda v, kk=kk: v.tensor_copy(
                rows1[0:1, 2050 + kk * 128:2050 + (kk + 1) * 128],
                pF[0:1, 0:128]), waits=[tm, rowff_guard[0]])
            read_psum(te, [7])
            rdma.append(te)
        tb1 = mm_group([6], [(pE[:, 0:TW], ones[0:1, 0:128],
                              rows1[0:1, 2050:2050 + TW],
                              dict(start=True, stop=True))],
                       extra_waits=rdma + [t_ones])
        tsB = ev("vector", lambda v: v.tensor_copy(sBb[:, :], pE[:, 0:TW]),
                 waits=[tb1])
        read_psum(tsB, [6])
        tb2 = mm_group([6], [(pE[:, 0:TW], ones[0:1, 0:128],
                              rows1[0:1, 2050 + TW:2050 + 2 * TW],
                              dict(start=True, stop=True))],
                       extra_waits=rdma)
        trs = ev("vector", lambda v: v.tensor_copy(rs2b[:, :], pE[:, 0:TW]),
                 waits=[tb2])
        read_psum(trs, [6])
        rowff_guard[0] = tb2
        n2_toks = []
        for ch in range(NCH):
            t_xq = ev("vector", lambda v, ch=ch: v.tensor_tensor(
                fftmp[:, 256:512], hs[:, ch * TW:(ch + 1) * TW], sBb[:, :],
                OP.mult), waits=[tsB, silu_last, ffA_guard[0], n2_guard[0]])
            t_i = ev("vector", lambda v: v.tensor_copy(fft_i[:, :],
                                                       fftmp[:, 256:512]),
                     waits=[t_xq])
            t_n = ev("vector", lambda v, ch=ch: v.tensor_copy(
                n2[:, ch * TW:(ch + 1) * TW], fft_i[:, :]), waits=[t_i])
            ffA_guard[0] = t_n
            n2_toks.append(t_n)
        hs_guard[0] = n2_toks[-1]
        for o in range(NCD):
            msl = o % 2
            mdma = dma("sync",
                       m2r[msl][:, :],
                       d_m2q[o * 128:(o + 1) * 128, :],
                       waits=[m2r_tok[msl]] +
                       (m2q_spills[-1:] + xsp_all
                        if (i == 0 and o < 2) else []),
                       stream=f"s_m2r{msl}")
            pt = pB[:, msl * 512:msl * 512 + TW]
            mms = [(pt, m2r[msl][:, k * 128:(k + 1) * 128],
                    n2[:, k * TW:(k + 1) * TW],
                    dict(start=(k == 0), stop=(k == NCH - 1)))
                   for k in range(NCH)]
            tg = mm_group([2 + msl], mms, extra_waits=[mdma] + n2_toks)
            m2r_tok[msl] = tg
            ty = ev("vector", lambda v, pt=pt: v.tensor_tensor(
                fftmp[:, 0:256], pt, rs2b[:, :], OP.mult),
                waits=[tg, trs, ffB_guard[0]])
            read_psum(ty, [2 + msl])
            to = ev("vector", lambda v, o=o, msl=msl: v.tensor_tensor(
                otile[:, msl * 256:(msl + 1) * 256], fftmp[:, 0:256],
                xfread[:, o * TW:(o + 1) * TW], OP.add),
                waits=[ty, xfr[o], ot_tok[msl]])
            ffB_guard[0] = to
            td = dma("gpsimd", d_yT[o * 128:(o + 1) * 128,
                                    i * TW:(i + 1) * TW],
                     otile[:, msl * 256:(msl + 1) * 256], waits=[to],
                     stream=f"s_out{msl}")
            ot_tok[msl] = td
            tok_out.append(td)
        n2_guard[0] = tg
        xfr_guard[0] = to

    quant_stage(0)
    quant_stage(1)
    for idx in range(32):
        m2q_spills += wq_passB(d_w2t, d_m2q, idx, 1, False, 3,
                               extra=list(c_w2) +
                               (xsp_all if idx < 2 else []))
    for i in range(NT):
        st = stitch(i)
        if i + 2 < NT:
            quant_stage(i + 2)
        ffn_tile(i, st)
        if "n1r0" in dumps and i == 0:
            dump_sbuf("n1r0", n1ring[0][:, :], n1_tok[0])
            dump_sbuf("hs0", hs[:, :], [hs_guard[0]])
            dump_sbuf("n20", n2[:, :], [hs_guard[0]])
        if "n2t2" in dumps and i == 2:
            dump_sbuf("n1r2", n1ring[2][:, :], n1_tok[2])
            dump_sbuf("n2t2", n2[:, :], [hs_guard[0], n2_guard[0]])
            dump_sbuf("rqb2", rqb[:, :], [hs_guard[0]])
            dump_sbuf("sbb2", sBb[:, :], [hs_guard[0]])
            dump_sbuf("rsb2", rs2b[:, :], [hs_guard[0]])
            hs_guard[0] = tok_out[-1]
            n2_guard[0] = tok_out[-1]

    dump_xsp("xspend", [ot_tok[0], ot_tok[1]])
    ev("gpsimd", lambda g: g.nop(), waits=tok_out + [ot_tok[0], ot_tok[1]])

    sems = {}
    for name in sorted(set(list(s.cnt.keys()))):
        sems[name] = nc.alloc_semaphore(name=name)
    nums = sorted(h.num for h in sems.values())
    assert nums == list(range(nums[0], nums[0] + len(nums)))
    nc.gpsimd.sem_clear(range(nums[0], nums[0] + len(nums)))
    nc.all_engine_barrier()
    with nc.Block() as block:

        def make_body(eng_name):
            def body(eng):
                waited = {}
                for (w, fn, sem, inc) in s.ops[eng_name]:
                    for (ws, wv) in w:
                        if waited.get(ws, -1) >= wv:
                            continue
                        eng.wait_ge(sems[ws], wv)
                        waited[ws] = wv
                    inst = fn(eng)
                    if inst is not None:
                        inst.then_inc(sems[sem], inc)
                    else:
                        eng.nop().then_inc(sems[sem], inc)
            return body

        block.sync(make_body("sync"))
        block.tensor(make_body("tensor"))
        block.vector(make_body("vector"))
        block.scalar(make_body("scalar"))
        block.gpsimd(make_body("gpsimd"))

    return nc, dump_outs


def _prep_inputs(inputs):
    x = np.asarray(inputs["L0_attn_x"], dtype=np.float32)
    B = x.shape[0]
    xT = np.ascontiguousarray(np.transpose(x, (0, 2, 1)))
    shared = {
        "wq": np.ascontiguousarray(inputs["L0_attn_Wq"], dtype=np.float32),
        "wk": np.ascontiguousarray(inputs["L0_attn_Wk"], dtype=np.float32),
        "wv": np.ascontiguousarray(inputs["L0_attn_Wv"], dtype=np.float32),
        "wo": np.ascontiguousarray(inputs["L0_attn_Wo"], dtype=np.float32),
        "bq": np.ascontiguousarray(inputs["L0_attn_bq"],
                                   dtype=np.float32).reshape(1, D),
        "bk": np.ascontiguousarray(inputs["L0_attn_bk"],
                                   dtype=np.float32).reshape(1, D),
        "bv": np.ascontiguousarray(inputs["L0_attn_bv"],
                                   dtype=np.float32).reshape(1, D),
        "bo": np.ascontiguousarray(inputs["L0_attn_bo"],
                                   dtype=np.float32).reshape(1, D),
        "w1t": np.ascontiguousarray(
            np.asarray(inputs["L0_ffn_w1"], dtype=np.float32).T),
        "cw": np.ascontiguousarray(
            np.asarray(inputs["L0_ffn_conv_w"], dtype=np.float32)[:, 0, :].T),
        "cb": np.ascontiguousarray(inputs["L0_ffn_conv_b"],
                                   dtype=np.float32).reshape(1, DH),
        "w2t": np.ascontiguousarray(
            np.asarray(inputs["L0_ffn_w2"], dtype=np.float32).T),
    }
    return xT, shared, B


def kernel(**inputs):
    xT, shared, B = _prep_inputs(inputs)
    S = xT.shape[2]
    key = ("prog", S)
    if key not in _CACHE:
        _CACHE[key] = build_program(S, paranoid=os.environ.get(
            "K_SCHED_MODE", "semi"))
    nc, _ = _CACHE[key]
    in_maps = [dict(shared, xT=xT[b]) for b in range(B)]
    res = run_bass_kernel_spmd(nc, in_maps, core_ids=list(range(B)))
    out = np.stack([np.ascontiguousarray(
        np.asarray(res.results[b]["yT"]).T) for b in range(B)], axis=0)
    return out


# version 14
# speedup vs baseline: 1.0052x; 1.0052x over previous
import os
import numpy as np
import ml_dtypes
import concourse.bass as bass
import concourse.mybir as mybir
from concourse.bass_utils import run_bass_kernel_spmd

F32 = mybir.dt.float32
BF16 = mybir.dt.bfloat16
I32 = mybir.dt.int32
AF = mybir.ActivationFunctionType
OP = mybir.AluOpType
AX = mybir.AxisListType

D = 1024
DH = 4096
NCD = D // 128
NCH = DH // 128
TW = 256
EPS_LN = 1e-5
EPS_Z = 1e-6
QCLIP = 1e-5

ENGINES = ("sync", "tensor", "vector", "scalar", "gpsimd")
_CACHE = {}


class Sched:
    def __init__(self, mode="fast"):
        if mode is True:
            mode = "paranoid"
        elif mode is False:
            mode = "fast"
        self.mode = mode
        self.ops = {e: [] for e in ENGINES}
        self.cnt = {}
        self.strict = set()

    def emit(self, eng, fn, waits=(), inc=1, dma=False, strict=False,
             stream=None):
        sem = (stream if stream else eng + "_dma") if dma else eng
        w = []
        for t in waits:
            if t is None:
                continue
            ts, tv = t
            if (self.mode == "fast" and ts == eng
                    and not ts.endswith("_dma")
                    and (ts, tv) not in self.strict):
                continue
            assert tv <= self.cnt.get(ts, 0), f"future wait {t}"
            w.append((ts, tv))
        if self.mode == "paranoid" and self.cnt.get(eng, 0) > 0:
            w.append((eng, self.cnt[eng]))
        self.ops[eng].append((w, fn, sem, inc))
        self.cnt[sem] = self.cnt.get(sem, 0) + inc
        tok = (sem, self.cnt[sem])
        if strict:
            self.strict.add(tok)
        return tok


def build_program(S, dumps=(), paranoid=False):
    NT = S // TW
    NSC = S // 128
    nc = bass.Bass("TRN2", target_bir_lowering=False, debug=False,
                   detect_race_conditions=(paranoid == "paranoid"
                                           or paranoid is True))

    d_xT = nc.dram_tensor("xT", [D, S], F32, kind="ExternalInput").ap()
    d_w = {nm: nc.dram_tensor(nm, [D, D], F32, kind="ExternalInput").ap()
           for nm in ("wq", "wk", "wv", "wo")}
    d_b = {nm: nc.dram_tensor(nm, [1, D], F32, kind="ExternalInput").ap()
           for nm in ("bq", "bk", "bv", "bo")}
    d_w1t = nc.dram_tensor("w1t", [D, DH], F32, kind="ExternalInput").ap()
    d_cw = nc.dram_tensor("cw", [3, DH], F32, kind="ExternalInput").ap()
    d_cb = nc.dram_tensor("cb", [1, DH], F32, kind="ExternalInput").ap()
    d_w2t = nc.dram_tensor("w2t", [DH, D], F32, kind="ExternalInput").ap()
    d_yT = nc.dram_tensor("yT", [D, S], F32, kind="ExternalOutput").ap()
    d_xsp = nc.dram_tensor("xsp", [D, S], F32, kind="Internal").ap()
    d_m1q = nc.dram_tensor("m1q", [8 * 8 * 128, 512], BF16, kind="Internal").ap()
    d_m2q = nc.dram_tensor("m2q", [8 * 128, 32 * 128], BF16,
                           kind="Internal").ap()
    d_ident = nc.inline_tensor(np.eye(128, dtype=np.float32), name="identc").ap()
    selc = np.zeros((16, 1024), dtype=np.float32)
    for c in range(8):
        selc[2 * c, c * 128:c * 128 + 64] = 1.0
        selc[2 * c + 1, c * 128 + 64:c * 128 + 128] = 1.0
    d_sel = nc.inline_tensor(selc, name="selc").ap()

    dump_outs = {}

    A = nc.alloc_sbuf_tensor
    rx = A("RX", [128, 65536], mybir.dt.uint8)
    rw = A("RW", [128, 65536], mybir.dt.uint8)
    RYC = 28832
    ry = A("RY", [128, RYC], mybir.dt.uint8)
    rr = A("RR", [128, 10272], mybir.dt.uint8)
    ws_r = A("WS", [128, 2048], mybir.dt.uint8)

    def place(backing, cap, name, shape, dtype, off):
        h = nc.virtual_sbuf_tensor(name, shape, dtype)
        nc._tensor(h.name, list(h.shape), dtype, type="SB")
        m = nc.lookup_mloc(h)
        m.allocated = not nc.target_bir_lowering
        m.addr = nc.lookup_mloc(backing).addr + off
        nbytes = int(np.prod(shape[1:])) * mybir.dt.size(dtype)
        assert off + nbytes <= cap, (name, off, nbytes)
        return h

    x_f = place(rx, 65536, "x_f", [128, NCD * S], F32, 0)
    hs = place(rx, 65536, "hs", [128, NCH * TW], F32, 0)
    n2 = place(rx, 65536, "n2", [128, NCH * TW], BF16, 32768)
    xfread = place(rx, 65536, "xfread", [128, NCD * TW], F32, 49152)
    h_in = [place(rx, 65536, f"h_in{i}", [128, 258], F32, 57344 + i * 1056)
            for i in range(2)]
    rqb = place(rx, 65536, "rqb", [128, 258], F32, 59456)
    sBb = place(rx, 65536, "sBb", [128, TW], F32, 60512)
    rs2b = place(rx, 65536, "rs2b", [128, TW], F32, 61568)
    M2 = place(rx, 65536, "M2", [128, TW], F32, 62624)
    gbuf = place(rx, 65536, "gbuf", [128, TW], F32, 63680)

    wA_hi = place(rw, 65536, "wA_hi", [128, NCD * D], BF16, 0)
    wA_lo = place(rw, 65536, "wA_lo", [128, NCD * D], BF16, 16384)
    wB_hi = place(rw, 65536, "wB_hi", [128, NCD * D], BF16, 32768)
    wB_lo = place(rw, 65536, "wB_lo", [128, NCD * D], BF16, 49152)
    m1r = [place(rw, 65536, f"m1r{i}", [128, 4096], BF16, i * 8192)
           for i in range(2)]
    m2r = [place(rw, 65536, f"m2r{i}", [128, 4096], BF16, 16384 + i * 8192)
           for i in range(2)]
    n1ring = [place(rw, 65536, f"n1ring{i}", [128, NCD * 258], BF16,
                    32768 + i * 4160) for i in range(3)]
    xqread = place(rw, 65536, "xqread", [128, NCD * TW], F32, 45248)
    otile = place(rw, 65536, "otile", [128, 512], F32, 53440)
    mbq2 = place(rw, 65536, "mbq2", [128, 1024], BF16, 55488)

    kp = place(ry, RYC, "kp", [128, 1024], F32, 0)
    vp = place(ry, RYC, "vp", [128, 1024], F32, 4096)
    qsb = place(ry, RYC, "qsb", [128, NCD * TW], F32, 0)
    xt_sub = place(ry, RYC, "xt_sub", [128, D], F32, 0)
    tq = place(ry, RYC, "tq", [128, D], F32, 4096)
    sim_safe = bool(int(os.environ.get("K_SIM_SAFE", "0")))
    ttmp = place(ry, RYC, "ttmp", [128, TW], F32, 8192)
    n1t_f = place(ry, RYC, "n1t_f", [128, D], F32, 8192)
    n1t_i = (A("n1t_i", [128, D], I32) if sim_safe else
             place(ry, RYC, "n1t_i", [128, D], I32, 8192))
    etmp = place(ry, RYC, "etmp", [128, 1024], F32, 9216)
    fftmp = place(ry, RYC, "fftmp", [128, 512], F32, 12288)
    fft_i = (A("fft_i", [128, 256], I32) if sim_safe else
             place(ry, RYC, "fft_i", [128, 256], I32, 13312))
    gbuf2 = place(ry, RYC, "gbuf2", [128, TW], F32, 14336)
    tanh2 = place(ry, RYC, "tanh2", [128, TW], F32, 15392)
    kvacc = place(ry, RYC, "kvacc", [128, 520], F32, 13312)
    rzT = place(ry, RYC, "rzT", [16, TW], F32, 15392)
    tmprow = place(ry, RYC, "tmprow", [32, 128], F32, 15392)
    tmp32 = place(ry, RYC, "tmp32", [32, 128], F32, 15904)
    rz_t = place(ry, RYC, "rz_t", [128, 32], F32, 16416)
    xn_hi = place(ry, RYC, "xn_hi", [128, NCD * TW], BF16, 16544)
    xn_lo = place(ry, RYC, "xn_lo", [128, NCD * TW], BF16, 20640)
    attn_hi = place(ry, RYC, "attn_hi", [128, NCD * TW], BF16, 16544)
    attn_lo = place(ry, RYC, "attn_lo", [128, NCD * TW], BF16, 20640)
    sel16 = place(ry, RYC, "sel16", [16, 1024], F32, 24736)

    rows1 = place(rr, 10272, "rows1", [128, 2568], F32, 0)
    rows2 = place(rr, 10272, "rows2", [128, 2048], F32, 0)
    wstage = place(ws_r, 2048, "wstage", [128, 512], F32, 0)
    wst_i = (A("wst_i", [128, 512], I32) if sim_safe else
             place(ws_r, 2048, "wst_i", [128, 512], I32, 0))

    ones = A("ones", [128, 256], F32)
    ident = A("ident", [128, 128], F32)
    aw = A("aw", [128, 512], F32)
    mbq = A("mbq", [128, 1024], BF16)
    wcols = A("wcols", [128, 128], F32)
    cells = A("cells", [128, 8], F32)
    scol = A("scol", [128, 48], F32)
    rqpair = A("rqpair", [128, 4], F32)
    mr_t = A("mr_t", [128, 2 * NSC], F32)
    csts = A("csts", [128, 4], F32)
    fence = A("zzfence", [128, 4], F32)
    fm = nc.lookup_mloc(fence)
    assert sim_safe or int(fm.addr) + 16 <= 196608, \
        ("SBUF overflow", int(fm.addr))
    nc.const_aps.register(F32, EPS_LN, csts[:, 0:1])
    nc.const_aps.register(F32, EPS_Z, csts[:, 1:2])
    nc.const_aps.register(F32, 1.0 / D, csts[:, 2:3])

    pA = nc.alloc_psum_tensor("pA", [128, 1024], F32)
    pB = nc.alloc_psum_tensor("pB", [128, 1024], F32)
    pC = nc.alloc_psum_tensor("pC", [128, 512], F32)
    pD = nc.alloc_psum_tensor("pD", [128, 512], F32)
    pE = nc.alloc_psum_tensor("pE", [128, 512], F32)
    pF = nc.alloc_psum_tensor("pF", [128, 512], F32)

    s = Sched(paranoid)
    psum_tok = {i: None for i in range(8)}

    def mm_group(banks, mms, extra_waits=()):
        waits = list(extra_waits) + [psum_tok[b] for b in banks]
        def fn(t, mms=mms):
            last = None
            for (o, l, r, kw) in mms:
                last = t.matmul(o, l, r, **kw)
            return last
        return s.emit("tensor", fn, waits=waits)

    def read_psum(tok, banks):
        for b in banks:
            psum_tok[b] = tok

    def ev(eng, fn, waits=(), strict=False):
        return s.emit(eng, fn, waits=waits, strict=strict)

    def dma(eng, out, in_, waits=(), stream=None):
        return s.emit(eng, (lambda e, o=out, i=in_: e.dma_start(out=o, in_=i)),
                      waits=waits, inc=16, dma=True, stream=stream)

    tok_out = []

    def dump_sbuf(name, ap, waits):
        if name in dumps:
            dd = nc.dram_tensor("dmp_" + name, list(ap.shape), ap.dtype,
                                kind="ExternalOutput").ap()
            dump_outs[name] = dd
            tok_out.append(dma("gpsimd", dd, ap, waits=waits))

    t_ones = ev("gpsimd", lambda g: g.memset(ones[:, :], 1.0))
    t_c0 = ev("gpsimd", lambda g: g.memset(csts[:, 0:1], EPS_LN))
    t_c1 = ev("gpsimd", lambda g: g.memset(csts[:, 1:2], EPS_Z))
    t_c2 = ev("gpsimd", lambda g: g.memset(csts[:, 2:3], 1.0 / D))
    t_csts = t_c2
    t_kv0 = ev("gpsimd", lambda g: g.memset(kvacc[:, :], 0.0))
    t_id = dma("gpsimd", ident[:, :], d_ident, stream="s_const")

    tx_l = [dma("sync", x_f[:, c * S:(c + 1) * S],
                d_xT[c * 128:(c + 1) * 128, :], stream="s_x")
            for c in range(NCD)]
    tx = tx_l

    _tb1 = dma("sync", rows2[64:65, 0:D], d_b["bk"], stream="s_b1")
    t_bv = dma("sync", rows2[64:65, D:2 * D], d_b["bv"], stream="s_b1")
    t_bk = t_bv

    last_wc = None
    for t3 in range(4):
        src = d_cw[t3:t3 + 1, :] if t3 < 3 else d_cb[0:1, :]
        tt = dma("sync", tmp32[:, :], src.rearrange("o (j p) -> (o j) p", p=128),
                 waits=[last_wc], stream="s_cw")
        tm = mm_group([7], [(pF[:, 0:32], tmp32[:, :], ident[0:32, 0:32],
                             dict(start=True, stop=True, is_transpose=True))],
                      extra_waits=[tt, t_id])
        last_wc = ev("vector", lambda v, t3=t3: v.tensor_copy(
            wcols[:, t3 * 32:(t3 + 1) * 32], pF[:, 0:32]), waits=[tm],
            strict=True)
        read_psum(last_wc, [7])
    last_wc = ev("vector", lambda v: v.tensor_scalar(
        wcols[:, 96:128], wcols[:, 96:128], 0.5, None, OP.mult),
        waits=[last_wc], strict=True)
    t_wcols = last_wc

    wst_tok = [None]

    def load_split_W(dram_ap, hi_t, lo_t, free_tok):
        last = None
        for c in range(NCD):
            for h2 in range(2):
                lo_c = c * 1024 + h2 * 512
                td = dma("scalar", wstage[:, :],
                         dram_ap[c * 128:(c + 1) * 128,
                                 h2 * 512:(h2 + 1) * 512],
                         waits=[wst_tok[0], free_tok], stream="s_ws")
                th = ev("vector", lambda v, lo_c=lo_c, hi=hi_t: v.tensor_copy(
                    hi[:, lo_c:lo_c + 512], wstage[:, :]),
                    waits=[td, free_tok])
                tl = ev("vector", lambda v, lo_c=lo_c, hi=hi_t, lo=lo_t:
                        v.tensor_tensor(lo[:, lo_c:lo_c + 512],
                                        wstage[:, :],
                                        hi[:, lo_c:lo_c + 512], OP.subtract),
                        waits=[th])
                wst_tok[0] = tl
                last = tl
        return last

    t_wk = load_split_W(d_w["wk"], wA_hi, wA_lo, None)
    t_wv = load_split_W(d_w["wv"], wB_hi, wB_lo, None)

    xt_guard = None
    xt_read = None
    t6a_prev = [None]
    mr_toks = []
    for sub in range(NSC):
        evs = []
        for c in range(NCD):
            pt, bk = (pC, 4) if c % 2 == 0 else (pD, 5)
            tm = mm_group([bk], [(pt[:, 0:128],
                                  x_f[:, c * S + sub * 128:c * S + sub * 128 + 128],
                                  ident[:, :],
                                  dict(start=True, stop=True, is_transpose=True))],
                          extra_waits=[tx[c], t_id])
            te = ev("vector", lambda v, c=c, pt=pt: v.tensor_copy(
                xt_sub[:, c * 128:(c + 1) * 128], pt[:, 0:128]),
                waits=[tm, xt_guard])
            read_psum(te, [bk])
            evs.append(te)
        t1 = ev("vector", lambda v: v.tensor_reduce(
            scol[:, 0:1], xt_sub[:, :], AX.X, OP.add), waits=evs, strict=True)
        t2 = ev("vector", lambda v: v.tensor_scalar(
            scol[:, 1:2], scol[:, 0:1], 1.0 / D, None, OP.mult), waits=[t1],
            strict=True)
        t3 = ev("vector", lambda v: v.tensor_scalar(
            tq[:, :], xt_sub[:, :], scol[:, 1:2], None, OP.subtract), waits=[t2])
        t4 = ev("vector", lambda v: v.tensor_tensor(
            xt_sub[:, :], tq[:, :], tq[:, :], OP.mult), waits=[t3])
        xt_guard = t4
        t5 = ev("vector", lambda v: v.tensor_reduce(
            scol[:, 2:3], xt_sub[:, :], AX.X, OP.add),
            waits=[t4, t6a_prev[0]], strict=True)
        xt_read = t5
        t6a = ev("scalar", lambda sc: sc.activation(
            scol[:, 4:5], scol[:, 2:3], AF.Sqrt, bias=EPS_LN, scale=1.0 / D),
            waits=[t5, t_csts])
        t6a_prev[0] = t6a
        t6 = ev("vector", lambda v: v.reciprocal(scol[:, 3:4], scol[:, 4:5]),
                waits=[t6a], strict=True)
        mr_toks.append(ev("vector", lambda v, sub=sub: v.tensor_copy(
            mr_t[:, 2 * sub:2 * sub + 1], scol[:, 1:2]), waits=[t2]))
        mr_toks.append(ev("vector", lambda v, sub=sub: v.tensor_copy(
            mr_t[:, 2 * sub + 1:2 * sub + 2], scol[:, 3:4]), waits=[t6]))
        if sub == 0:
            dump_sbuf("scol0", scol[:, 0:8], [mr_toks[-1], t6, t5])
            dump_sbuf("xts0", xt_sub[:, :], [t5])
            dump_sbuf("tq0", tq[:, :], [t5, t4])
    tmprow_guard = [None]
    dump_sbuf("mrt", mr_t[:, :], [mr_toks[-1]])

    xn_last_use = [None]
    etmp_guard = [None]
    rowxn_guard = [None]

    def build_xn(a, extra=()):
        row_toks = []
        for which in range(2):
            for jj in range(2):
                col = 2 * (2 * a + jj) + which
                tm = mm_group([6], [(pE[0:1, 0:128],
                                     mr_t[:, col:col + 1], ident[:, :],
                                     dict(start=True, stop=True,
                                          is_transpose=True))],
                              extra_waits=[mr_toks[col], t_id] + list(extra))
                te = ev("vector", lambda v, which=which, jj=jj: v.tensor_copy(
                    rows1[0:1, which * 256 + jj * 128:
                          which * 256 + (jj + 1) * 128],
                    pE[0:1, 0:128]), waits=[tm, rowxn_guard[0]])
                read_psum(te, [6])
                row_toks.append(te)
        tb1 = mm_group([6], [(pE[:, 0:TW], ones[0:1, 0:128],
                              rows1[0:1, 0:TW],
                              dict(start=True, stop=True))],
                       extra_waits=row_toks + [t_ones] + list(extra))
        tb2 = mm_group([5], [(pD[:, 0:TW], ones[0:1, 0:128],
                              rows1[0:1, TW:2 * TW],
                              dict(start=True, stop=True))],
                       extra_waits=row_toks + [t_ones])
        rowxn_guard[0] = tb2
        toks = []
        prev = xn_last_use[0]
        last_t1 = None
        last_t2 = None
        for c in range(NCD):
            t1 = ev("vector", lambda v, c=c, a=a: v.tensor_tensor(
                ttmp[:, :],
                x_f[:, c * S + a * TW:c * S + (a + 1) * TW],
                pE[:, 0:TW], OP.subtract), waits=[tb1, tx[c], prev])
            t2 = ev("vector", lambda v: v.tensor_tensor(
                ttmp[:, :], ttmp[:, :],
                pD[:, 0:TW], OP.mult), waits=[tb2])
            t3 = ev("vector", lambda v, c=c: v.tensor_copy(
                xn_hi[:, c * TW:(c + 1) * TW], ttmp[:, :]),
                waits=[t2])
            t4 = ev("vector", lambda v, c=c: v.tensor_tensor(
                xn_lo[:, c * TW:(c + 1) * TW], ttmp[:, :],
                xn_hi[:, c * TW:(c + 1) * TW], OP.subtract), waits=[t3])
            toks.append(t4)
            last_t1 = t1
            last_t2 = t2
        read_psum(last_t1, [6])
        read_psum(last_t2, [5])
        return toks

    aw_guard = [None]

    def wq_passA(dram, idx, n_pieces, npiece_cols, extra=()):
        rc, cb = idx // npiece_cols, idx % npiece_cols
        tg = None
        for h2 in range(2):
            td = dma("sync", wstage[:, :],
                     dram[rc * 128:(rc + 1) * 128,
                          cb * 1024 + h2 * 512:cb * 1024 + (h2 + 1) * 512],
                     waits=[wst_tok[0]] + list(extra), stream="s_ws")
            ta = ev("vector", lambda v: v.scalar_tensor_tensor(
                aw[:, :], wstage[:, :], -1.0, wstage[:, :], OP.mult, OP.max),
                waits=[td, aw_guard[0]])
            wst_tok[0] = ta
            tg = mm_group([7], [(pF[0:1, 0:512], ones[:, 0:1], aw[:, :],
                                 dict(start=(idx == 0 and h2 == 0),
                                      stop=(idx == n_pieces - 1 and h2 == 1)))],
                          extra_waits=[ta, t_ones])
            aw_guard[0] = tg
        return tg

    def wq_finalize(nelem, ci_scale, ci_recip, last_tok):
        te = ev("vector", lambda v: v.tensor_copy(tmprow[0:1, 0:128],
                                                  pF[0:1, 0:128]),
                waits=[last_tok] + tmprow_guard)
        acc = te
        for kpart in range(1, 4):
            tc = ev("vector", lambda v, kpart=kpart: v.tensor_copy(
                tmp32[0:1, 0:128], pF[0:1, kpart * 128:(kpart + 1) * 128]),
                waits=[acc])
            acc = ev("vector", lambda v: v.tensor_tensor(
                tmprow[0:1, 0:128], tmprow[0:1, 0:128], tmp32[0:1, 0:128],
                OP.add), waits=[tc])
        read_psum(acc, [7])
        tr = ev("vector", lambda v, ci=ci_scale: v.tensor_reduce(
            rows1[0:1, 2562 + ci:2563 + ci], tmprow[0:1, 0:128], AX.X, OP.add),
            waits=[acc], strict=True)
        tm2 = ev("vector", lambda v, ci=ci_scale, nelem=nelem: v.tensor_scalar(
            rows1[0:1, 2562 + ci:2563 + ci], rows1[0:1, 2562 + ci:2563 + ci],
            1.0 / nelem, QCLIP, OP.mult, OP.max), waits=[tr])
        tb = mm_group([7], [(pF[:, 0:1], ones[0:1, 0:128],
                             rows1[0:1, 2562 + ci_scale:2563 + ci_scale],
                             dict(start=True, stop=True))],
                      extra_waits=[tm2, t_ones])
        tc1 = ev("vector", lambda v, ci=ci_scale: v.tensor_scalar(
            cells[:, ci:ci + 1], pF[:, 0:1], 1.0 / 127.0, None, OP.mult),
            waits=[tb], strict=True)
        tc2a = ev("vector", lambda v: v.tensor_copy(cells[:, 7:8],
                                                    pF[:, 0:1]), waits=[tb])
        tc2 = ev("vector", lambda v, ci=ci_recip: v.reciprocal(
            cells[:, ci:ci + 1], cells[:, 7:8]), waits=[tc2a], strict=True)
        read_psum(tc2, [7])
        tmprow_guard[:] = [tr]
        return (tc1, tc2)

    mi32_guard = [None]
    mbq_guard = [None, None]

    def wq_passB(dram, dspill, idx, npiece_cols, is_w1, ci_recip, extra=()):
        rc, cb = idx // npiece_cols, idx % npiece_cols
        mb = mbq if (is_w1 or idx % 2 == 0) else mbq2
        gslot = 0 if (is_w1 or idx % 2 == 0) else 1
        toks = []
        tcl_l = []
        for half in range(2):
            td = dma("sync", wstage[:, :],
                     dram[rc * 128:(rc + 1) * 128,
                          cb * 1024 + half * 512:cb * 1024 + (half + 1) * 512],
                     waits=[wst_tok[0]] + list(extra), stream="s_ws")
            ti = ev("vector", lambda v, ci=ci_recip: v.tensor_scalar(
                wst_i[:, :], wstage[:, :], cells[:, ci:ci + 1], None, OP.mult),
                waits=[td])
            tc = ev("vector", lambda v, half=half, mb=mb: v.tensor_copy(
                mb[:, half * 512:(half + 1) * 512], wst_i[:, :]),
                waits=[ti] + ([mbq_guard[gslot]] if mbq_guard[gslot] else []))
            wst_tok[0] = tc
            tcl = ev("vector", lambda v, half=half, mb=mb: v.tensor_scalar(
                mb[:, half * 512:(half + 1) * 512],
                mb[:, half * 512:(half + 1) * 512],
                1.0, -1.0, OP.min, OP.max), waits=[tc])
            tcl_l.append(tcl)
            if is_w1:
                hb = 2 * cb + half
                toks.append(dma(
                    "scalar",
                    dspill[(hb * 8 + rc) * 128:(hb * 8 + rc + 1) * 128, :],
                    mb[:, half * 512:(half + 1) * 512], waits=[tcl],
                    stream="s_m1sp"))
        if not is_w1:
            for ob in range(8):
                toks.append(dma(
                    "scalar",
                    dspill[ob * 128:(ob + 1) * 128,
                           rc * 128:(rc + 1) * 128],
                    mb[:, ob * 128:(ob + 1) * 128], waits=tcl_l,
                    stream="s_m2sp"))
        mbq_guard[gslot] = toks[-1]
        return toks

    kv_read_tok = None
    kv_add_tok = None
    w1a_i = [0]
    w1a_last = [None]
    p1_last_mm = None
    for a in range(NT):
        xn_toks = build_xn(a)
        for j in range(2):
            sc = a * 2 + j
            kgroups = []
            vgroups = []
            for (proj, whi, wlo, wtok, btok) in (
                    ("k", wA_hi, wA_lo, t_wk, t_bk),
                    ("v", wB_hi, wB_lo, t_wv, t_bv)):
                pt = pA if proj == "k" else pB
                bks = (0, 1) if proj == "k" else (2, 3)
                for n in range(2):
                    mms = []
                    first = True
                    for (xs, ws) in ((xn_hi, whi), (xn_lo, whi), (xn_hi, wlo)):
                        for c in range(NCD):
                            mms.append((pt[:, n * 512:(n + 1) * 512],
                                        xs[:, c * TW + j * 128:
                                           c * TW + j * 128 + 128],
                                        ws[:, c * D + n * 512:
                                           c * D + (n + 1) * 512],
                                        dict(start=first, stop=False)))
                            first = False
                    bones = ones[64:65, 0:128]
                    brow = (rows2[64:65, n * 512:(n + 1) * 512] if proj == "k"
                            else rows2[64:65, D + n * 512:D + (n + 1) * 512])
                    mms.append((pt[:, n * 512:(n + 1) * 512], bones, brow,
                                dict(start=False, stop=True,
                                     tile_position=(64, 0))))
                    tg = mm_group([bks[n]], mms,
                                  extra_waits=xn_toks + [wtok, btok, t_ones,
                                                         kv_read_tok])
                    (kgroups if proj == "k" else vgroups).append(tg)
            tmin = ev("vector", lambda v: v.tensor_scalar(
                etmp[:, 0:1024], pA[:, 0:1024], 0.0, None, OP.min),
                waits=kgroups + [etmp_guard[0]])
            texp = ev("scalar", lambda sc_: sc_.activation(
                kp[:, :], etmp[:, 0:1024], AF.Exp), waits=[tmin])
            etmp_guard[0] = texp
            tk = ev("vector", lambda v: v.scalar_tensor_tensor(
                kp[:, :], pA[:, 0:1024], 0.0, kp[:, :],
                OP.max, OP.add), waits=[texp])
            read_psum(tk, [0, 1])
            tv = ev("vector", lambda v: v.tensor_copy(vp[:, :], pB[:, 0:1024]),
                    waits=vgroups)
            read_psum(tv, [2, 3])
            kvg = []
            for half in range(2):
                pt, bk = (pC, 4) if half == 0 else (pD, 5)
                mms = [(pt[:, 500:501], ones[:, 0:128], ones[:, 0:1],
                        dict(start=True, stop=False))]
                for pp in range(4):
                    p = half * 4 + pp
                    for parity in range(2):
                        h = 2 * p + parity
                        base = parity * 64
                        kw = dict(start=False, stop=False)
                        if parity:
                            kw["tile_position"] = (0, 64)
                        mms.append((pt[base:base + 64, pp * 65:pp * 65 + 64],
                                    kp[:, h * 64:(h + 1) * 64],
                                    vp[:, h * 64:(h + 1) * 64], kw))
                        kw2 = dict(start=False, stop=False)
                        if parity:
                            kw2["tile_position"] = (0, 64)
                        mms.append((pt[base:base + 64,
                                       pp * 65 + 64:pp * 65 + 65],
                                    kp[:, h * 64:(h + 1) * 64],
                                    ones[:, 0:1], kw2))
                mms.append((pt[:, 500:501], ones[:, 0:128], ones[:, 0:1],
                            dict(start=False, stop=True)))
                kvg.append(mm_group([bk], mms, extra_waits=[tk, tv, t_ones]))
            ta1 = ev("vector", lambda v: v.tensor_tensor(
                kvacc[:, 0:260], kvacc[:, 0:260], pC[:, 0:260], OP.add),
                waits=[kvg[0], t_kv0, kv_add_tok])
            read_psum(ta1, [4])
            ta2 = ev("vector", lambda v: v.tensor_tensor(
                kvacc[:, 260:520], kvacc[:, 260:520], pD[:, 0:260], OP.add),
                waits=[kvg[1], t_kv0, ta1])
            read_psum(ta2, [5])
            kv_add_tok = ta2
            kv_read_tok = kvg[1]
            p1_last_mm = kvg[1]
            xn_last_use[0] = kvg[1]
            for _ in range(2):
                if w1a_i[0] < 32:
                    w1a_last[0] = wq_passA(d_w1t, w1a_i[0], 32, 4)
                    w1a_i[0] += 1
    while w1a_i[0] < 32:
        w1a_last[0] = wq_passA(d_w1t, w1a_i[0], 32, 4)
        w1a_i[0] += 1
    dump_sbuf("kvacc", kvacc[:, :], [kv_add_tok])

    c_w1 = wq_finalize(D * DH, 0, 1, w1a_last[0])

    t_wq = load_split_W(d_w["wq"], wA_hi, wA_lo, p1_last_mm)
    t_wo = load_split_W(d_w["wo"], wB_hi, wB_lo, p1_last_mm)
    _tb2 = dma("sync", rows2[64:65, 0:D], d_b["bq"], waits=[p1_last_mm],
               stream="s_b2")
    t_bo = dma("sync", rows2[64:65, D:2 * D], d_b["bo"], waits=[p1_last_mm],
               stream="s_b2")
    t_bq = t_bo
    t_sel = dma("gpsimd", sel16[:, :], d_sel, waits=[p1_last_mm, tk],
                stream="s_sel")

    w1b_i = [0]
    w2a_i = [0]
    w2a_last = [None]
    m1q_spills = []
    attn_guard = None
    xsp_toks = {}
    rz_guard = [None]
    rzT_guard = [c_w1[1]]
    for a in range(NT):
        xn_toks = build_xn(a, extra=[t_wq])
        q_toks = []
        for o in range(NCD):
            half = o % 2
            pq = pA[:, half * 512:half * 512 + TW]
            bk = half
            mms = []
            first = True
            for (xs, ws) in ((xn_hi, wA_hi), (xn_lo, wA_hi), (xn_hi, wA_lo)):
                for c in range(NCD):
                    mms.append((pq, ws[:, c * D + o * 128:c * D + (o + 1) * 128],
                                xs[:, c * TW:(c + 1) * TW],
                                dict(start=first, stop=False)))
                    first = False
            mms.append((pq, rows2[64:65, o * 128:(o + 1) * 128],
                        ones[64:65, 0:TW], dict(start=False, stop=True,
                                                tile_position=(64, 0))))
            tg = mm_group([bk], mms, extra_waits=xn_toks + [t_wq, t_bq, t_ones])
            tmin = ev("vector", lambda v, pq=pq: v.tensor_scalar(
                etmp[:, 0:TW], pq, 0.0, None, OP.min),
                waits=[tg, etmp_guard[0]])
            texp = ev("scalar", lambda sc_: sc_.activation(
                etmp[:, TW:2 * TW], etmp[:, 0:TW], AF.Exp), waits=[tmin])
            etmp_guard[0] = texp
            tq2 = ev("vector", lambda v, o=o, pq=pq: v.scalar_tensor_tensor(
                qsb[:, o * TW:(o + 1) * TW], pq, 0.0, etmp[:, TW:2 * TW],
                OP.max, OP.add), waits=[texp])
            read_psum(tq2, [bk])
            q_toks.append(tq2)
            xn_last_use[0] = tg
        rz_toks = []
        for j in range(2):
            mms = []
            for h in range(16):
                p = h // 2
                base = (h % 2) * 64
                kwz = dict(start=(h == 0), stop=(h == 15))
                if base:
                    kwz["tile_position"] = (64, 0)
                mms.append((pB[:, j * 16 + h:j * 16 + h + 1],
                            qsb[base:base + 64,
                                p * TW + j * 128:p * TW + j * 128 + 128],
                            kvacc[base:base + 64, p * 65 + 64:p * 65 + 65],
                            kwz))
            tzg = mm_group([2], mms,
                           extra_waits=q_toks + [kv_add_tok] + rz_toks)
            tze = ev("vector", lambda v, j=j: v.tensor_scalar(
                scol[:, 0:16], pB[:, j * 16:(j + 1) * 16], EPS_Z, None,
                OP.add), waits=[tzg, rz_guard[0]])
            trz = ev("vector", lambda v, j=j: v.reciprocal(
                rz_t[:, j * 16:(j + 1) * 16], scol[:, 0:16]), waits=[tze],
                strict=True)
            read_psum(trz, [2])
            rz_toks.append(trz)
        rzT_toks = []
        for j in range(2):
            tm = mm_group([3], [(pB[0:16, 512:640], rz_t[:, j * 16:(j + 1) * 16],
                                 ident[:, :],
                                 dict(start=True, stop=True,
                                      is_transpose=True))],
                          extra_waits=[rz_toks[j], t_id])
            te = ev("vector", lambda v, j=j: v.tensor_copy(
                rzT[0:16, j * 128:(j + 1) * 128], pB[0:16, 512:640]),
                waits=[tm, rzT_guard[0]])
            read_psum(te, [3])
            rzT_toks.append(te)
        rz_guard[0] = tm
        qpp_toks = []
        for c in range(NCD):
            tm = mm_group([3], [(pB[:, 512:512 + TW],
                                 sel16[0:16, c * 128:(c + 1) * 128],
                                 rzT[0:16, 0:TW], dict(start=True, stop=True))],
                          extra_waits=rzT_toks + [t_sel])
            tqq = ev("vector", lambda v, c=c: v.tensor_tensor(
                qsb[:, c * TW:(c + 1) * TW], qsb[:, c * TW:(c + 1) * TW],
                pB[:, 512:512 + TW], OP.mult), waits=[tm])
            read_psum(tqq, [3])
            qpp_toks.append(tqq)
        rzT_guard[0] = tm
        pair_toks = []
        for p in range(8):
            pt, bk = (pC, 4) if p % 2 == 0 else (pD, 5)
            mms = [(pt[:, 400:401], ones[:, 0:128], ones[:, 0:1],
                    dict(start=True, stop=False))]
            for parity in range(2):
                h = 2 * p + parity
                base = parity * 64
                kw = dict(start=False, stop=False)
                if parity:
                    kw["tile_position"] = (64, 64)
                mms.append((pt[base:base + 64, 0:TW],
                            kvacc[base:base + 64, p * 65:p * 65 + 64],
                            qsb[base:base + 64, p * TW:(p + 1) * TW], kw))
            mms.append((pt[:, 400:401], ones[:, 0:128], ones[:, 0:1],
                        dict(start=False, stop=True)))
            tg = mm_group([bk], mms,
                          extra_waits=[qpp_toks[p], kv_add_tok, attn_guard])
            th = ev("vector", lambda v, p=p, pt=pt: v.tensor_copy(
                attn_hi[:, p * TW:(p + 1) * TW], pt[:, 0:TW]),
                waits=[tg])
            tl = ev("vector", lambda v, p=p, pt=pt: v.tensor_tensor(
                attn_lo[:, p * TW:(p + 1) * TW], pt[:, 0:TW],
                attn_hi[:, p * TW:(p + 1) * TW], OP.subtract), waits=[th])
            read_psum(tl, [bk])
            pair_toks.append(tl)
        for o in range(NCD):
            mms = []
            first = True
            for (xs, ws) in ((attn_hi, wB_hi), (attn_lo, wB_hi),
                             (attn_hi, wB_lo)):
                for c in range(NCD):
                    mms.append((pE[:, 0:TW],
                                ws[:, c * D + o * 128:c * D + (o + 1) * 128],
                                xs[:, c * TW:(c + 1) * TW],
                                dict(start=first, stop=False)))
                    first = False
            mms.append((pE[:, 0:TW],
                        rows2[64:65, D + o * 128:D + (o + 1) * 128],
                        ones[64:65, 0:TW], dict(start=False, stop=True,
                                                tile_position=(64, 0))))
            tg = mm_group([6], mms,
                          extra_waits=pair_toks + [t_wo, t_bo, t_ones])
            tex = ev("vector", lambda v, o=o, a=a: v.tensor_tensor(
                x_f[:, o * S + a * TW:o * S + (a + 1) * TW],
                x_f[:, o * S + a * TW:o * S + (a + 1) * TW],
                pE[:, 0:TW], OP.add), waits=[tg])
            read_psum(tex, [6])
            attn_guard = tg
            if o == NCD - 1:
                wo_last = tex
            if o == 0:
                xsp_toks[a] = []
            xsp_toks[a].append(dma(
                "gpsimd", d_xsp[o * 128:(o + 1) * 128, a * TW:(a + 1) * TW],
                x_f[:, o * S + a * TW:o * S + (a + 1) * TW], waits=[tex],
                stream="s_xsp"))
        xn_last_use[0] = attn_guard
        for _ in range(4):
            if w1b_i[0] < 32:
                m1q_spills += wq_passB(d_w1t, d_m1q, w1b_i[0], 4, True, 1,
                                       extra=list(c_w1))
                w1b_i[0] += 1
        for _ in range(4):
            if w2a_i[0] < 32:
                w2a_last[0] = wq_passA(d_w2t, w2a_i[0], 32, 1,
                                       extra=[c_w1[0]])
                w2a_i[0] += 1
    while w1b_i[0] < 32:
        m1q_spills += wq_passB(d_w1t, d_m1q, w1b_i[0], 4, True, 1,
                               extra=list(c_w1))
        w1b_i[0] += 1
    while w2a_i[0] < 32:
        w2a_last[0] = wq_passA(d_w2t, w2a_i[0], 32, 1, extra=[c_w1[0]])
        w2a_i[0] += 1
    dump_sbuf("xattn", x_f[:, :], [wo_last])
    dump_sbuf("qpp", qsb[:, :], qpp_toks)
    dump_sbuf("attnhi", attn_hi[:, :], pair_toks)
    dump_sbuf("rzt", rz_t[:, :], rz_toks)

    tmprow_guard.append(rzT_guard[0])
    c_w2 = wq_finalize(DH * D, 2, 3, w2a_last[0])

    m2q_spills = []

    xsp_all = [xsp_toks[NT - 1][-1]]
    t_rq0 = ev("gpsimd", lambda g: g.memset(rows1[0:1, 0:1], 0.0),
               waits=xsp_all)
    t_rq1 = ev("gpsimd", lambda g: g.memset(rows1[0:1, S + 1:S + 2], 0.0),
               waits=xsp_all)

    def dump_xsp(name, waits):
        if name not in dumps:
            return
        dd = nc.dram_tensor("dmp_" + name, [128, 2048], F32,
                            kind="ExternalOutput").ap()
        dump_outs[name] = dd
        prev = list(waits) + [wst_tok[0]]
        for half in range(4):
            tl = dma("gpsimd", wstage[:, :],
                     d_xsp[0:128, half * 512:(half + 1) * 512],
                     waits=prev, stream="s_xdbg")
            ts_ = dma("gpsimd", dd[:, half * 512:(half + 1) * 512],
                      wstage[:, :], waits=[tl], stream="s_xdbg")
            prev = [ts_]
        tok_out.append(prev[0])
        wst_tok[0] = prev[0]

    dump_xsp("xspearly", xsp_all + m2q_spills[-1:])
    xf0_tok = []
    if "xf0" in dumps:
        dd0 = nc.dram_tensor("dmp_xf0", [128, 2048], F32,
                             kind="ExternalOutput").ap()
        dump_outs["xf0"] = dd0
        xf0_tok.append(dma("gpsimd", dd0, x_f[:, 0:2048],
                           waits=[wo_last] + xsp_all, stream="s_xdbg"))
        tok_out.append(xf0_tok[0])
    n1_tok = {}
    n1_slot_use = {}
    rq_row_toks = {}
    last_q_stage = [None]
    xtq_guard = [None]
    tq_guard = [None]
    scol_guard = {0: None, 1: None}
    t6a_guard = {0: None, 1: None}
    rqp_guard = [None]

    def quant_stage(i):
        sl = i % 3
        xr_l = [dma("sync", xqread[:, c * TW:(c + 1) * TW],
                    d_xsp[c * 128:(c + 1) * 128, i * TW:(i + 1) * TW],
                    waits=xsp_all + [last_q_stage[0]] + xf0_tok,
                    stream="s_xq")
                for c in range(NCD)]
        xr = [xr_l[-1]] * NCD
        sub_toks = []
        rqp_toks = []
        for j in range(2):
            evs = []
            for c in range(NCD):
                pt, bk = (pC, 4) if c % 2 == 0 else (pD, 5)
                tm = mm_group([bk], [(pt[:, 0:128],
                                      xqread[:, c * TW + j * 128:
                                             c * TW + j * 128 + 128],
                                      ident[:, :],
                                      dict(start=True, stop=True,
                                           is_transpose=True))],
                              extra_waits=[xr[c], t_id])
                te = ev("vector", lambda v, c=c, pt=pt: v.tensor_copy(
                    xt_sub[:, c * 128:(c + 1) * 128], pt[:, 0:128]),
                    waits=[tm, xtq_guard[0], last_q_stage[0]])
                read_psum(te, [bk])
                evs.append(te)
            sb = j * 16
            t1 = ev("vector", lambda v, sb=sb: v.tensor_reduce(
                scol[:, sb:sb + 1], xt_sub[:, :], AX.X, OP.add), waits=evs,
                strict=True)
            t2 = ev("vector", lambda v, sb=sb: v.tensor_scalar(
                scol[:, sb + 1:sb + 2], scol[:, sb:sb + 1], 1.0 / D, None,
                OP.mult), waits=[t1], strict=True)
            t3 = ev("vector", lambda v, sb=sb: v.tensor_scalar(
                tq[:, :], xt_sub[:, :], scol[:, sb + 1:sb + 2], None,
                OP.subtract), waits=[t2, tq_guard[0]])
            t4 = ev("vector", lambda v: v.tensor_tensor(
                xt_sub[:, :], tq[:, :], tq[:, :], OP.mult),
                waits=[t3, scol_guard[j]])
            xtq_guard[0] = t4
            t5 = ev("vector", lambda v, sb=sb: v.tensor_reduce(
                scol[:, sb + 2:sb + 3], xt_sub[:, :], AX.X, OP.add),
                waits=[t4, t6a_guard[j]], strict=True)
            t6a = ev("scalar", lambda sc_, sb=sb: sc_.activation(
                scol[:, sb + 10:sb + 11], scol[:, sb + 2:sb + 3], AF.Sqrt,
                bias=EPS_LN, scale=1.0 / D), waits=[t5, scol_guard[j], t_csts])
            t6a_guard[j] = t6a
            t6 = ev("vector", lambda v, sb=sb: v.reciprocal(
                scol[:, sb + 3:sb + 4], scol[:, sb + 10:sb + 11]),
                waits=[t6a], strict=True)
            t7 = ev("vector", lambda v, sb=sb: v.tensor_reduce(
                scol[:, sb + 4:sb + 5], tq[:, :], AX.X, OP.max,
                apply_absolute_value=True), waits=[t3], strict=True)
            t8 = ev("vector", lambda v, sb=sb: v.tensor_tensor(
                scol[:, sb + 5:sb + 6], scol[:, sb + 4:sb + 5],
                scol[:, sb + 3:sb + 4], OP.mult), waits=[t6, t7])
            t9 = ev("vector", lambda v, sb=sb: v.tensor_scalar(
                scol[:, sb + 6:sb + 7], scol[:, sb + 5:sb + 6], QCLIP, None,
                OP.max), waits=[t8])
            t10 = ev("vector", lambda v, sb=sb: v.reciprocal(
                scol[:, sb + 7:sb + 8], scol[:, sb + 6:sb + 7]), waits=[t9],
                strict=True)
            t11 = ev("vector", lambda v, sb=sb: v.tensor_scalar(
                scol[:, sb + 8:sb + 9], scol[:, sb + 7:sb + 8], 127.0, None,
                OP.mult), waits=[t10])
            t12 = ev("vector", lambda v, sb=sb: v.tensor_tensor(
                scol[:, sb + 9:sb + 10], scol[:, sb + 3:sb + 4],
                scol[:, sb + 8:sb + 9], OP.mult), waits=[t11], strict=True)
            t13 = ev("vector", lambda v, j=j, sb=sb: v.tensor_tensor(
                rqpair[:, j:j + 1], scol[:, sb + 6:sb + 7], cells[:, 0:1],
                OP.mult), waits=[t9, c_w1[0], rqp_guard[0]])
            rqp_toks.append(t13)
            ti = ev("vector", lambda v, sb=sb: v.tensor_scalar(
                n1t_i[:, :], tq[:, :], scol[:, sb + 9:sb + 10], None, OP.mult),
                waits=[t12, mi32_guard[0]])
            tq_guard[0] = ti
            scol_guard[j] = ti
            tb = ev("vector", lambda v: v.tensor_copy(n1t_f[:, :],
                                                      n1t_i[:, :]), waits=[ti])
            for c in range(NCD):
                pt, bk = (pC, 4) if c % 2 == 0 else (pD, 5)
                tm = mm_group([bk], [(pt[:, 0:128],
                                      n1t_f[:, c * 128:(c + 1) * 128],
                                      ident[:, :],
                                      dict(start=True, stop=True,
                                           is_transpose=True))],
                              extra_waits=[tb, t_id])
                te = ev("vector", lambda v, c=c, j=j, sl=sl, pt=pt:
                        v.tensor_copy(
                            n1ring[sl][:, c * 258 + 1 + j * 128:
                                       c * 258 + 1 + (j + 1) * 128],
                            pt[:, 0:128]),
                        waits=[tm, n1_slot_use.get(sl)])
                read_psum(te, [bk])
                sub_toks.append(te)
            mi32_guard[0] = sub_toks[-1]
        rq_row_toks[i] = []
        for j in range(2):
            tm = mm_group([7], [(pF[0:1, 0:128], rqpair[:, j:j + 1],
                                 ident[:, :],
                                 dict(start=True, stop=True,
                                      is_transpose=True))],
                          extra_waits=[rqp_toks[j], t_id])
            rqp_guard[0] = tm
            te = ev("vector", lambda v, i=i, j=j: v.tensor_copy(
                rows1[0:1, i * TW + j * 128 + 1:i * TW + (j + 1) * 128 + 1],
                pF[0:1, 0:128]), waits=[tm])
            read_psum(te, [7])
            rq_row_toks[i].append(te)
        if "xq2" in dumps and i == 2:
            dump_sbuf("xq2", xqread[:, :], sub_toks)
            dump_sbuf("scol2", scol[:, 0:32], sub_toks)
            last_q_stage[0] = tok_out[-1]
        n1_tok[i] = sub_toks
        if last_q_stage[0] is None or i != 2 or "xq2" not in dumps:
            last_q_stage[0] = sub_toks[-1]

    def stitch(i):
        sl = i % 3
        nr = n1ring[sl].ap().rearrange("p (c w) -> p c w", w=258)
        toks = []
        if i == 0:
            toks.append(ev("vector", lambda v, nr=nr: v.memset(nr[:, :, 0:1],
                                                               0.0),
                           waits=n1_tok[0]))
        else:
            nrp = n1ring[(i - 1) % 3].ap().rearrange("p (c w) -> p c w", w=258)
            toks.append(ev("vector", lambda v, nr=nr, nrp=nrp: v.tensor_copy(
                nr[:, :, 0:1], nrp[:, :, 256:257]),
                waits=n1_tok[i] + n1_tok[i - 1]))
        if i == NT - 1:
            toks.append(ev("vector", lambda v, nr=nr: v.memset(
                nr[:, :, 257:258], 0.0), waits=n1_tok[i]))
        else:
            nrn = n1ring[(i + 1) % 3].ap().rearrange("p (c w) -> p c w", w=258)
            toks.append(ev("vector", lambda v, nr=nr, nrn=nrn: v.tensor_copy(
                nr[:, :, 257:258], nrn[:, :, 1:2]),
                waits=n1_tok[i] + n1_tok[i + 1]))
        return toks

    m1r_tok = [None, None]
    m2r_tok = [None, None]
    hin_tok = [None, None]
    gb_tok = {0: None, 1: None}
    tb_guard = {0: None, 1: None}
    hs_guard = [None]
    M2_guard = [None]
    n2_guard = [None]
    ffA_guard = [None]
    ffB_guard = [None]
    ot_tok = [None, None]
    xfr_guard = [None]
    rowff_guard = [None]

    def ffn_tile(i, st_toks):
        sl = i % 3
        pend = []
        tb_guard[0] = ffA_guard[0]
        if tb_guard[1] is None:
            tb_guard[1] = c_w2[1]
        extra_rq = rq_row_toks[i] + (rq_row_toks.get(i + 1, []) or [t_rq1])
        tbq = mm_group([6], [(pE[:, 0:258], ones[0:1, 0:128],
                              rows1[0:1, i * TW:i * TW + 258],
                              dict(start=True, stop=True))],
                       extra_waits=extra_rq + [t_rq0, t_rq1, t_ones])
        trq = ev("vector", lambda v: v.tensor_copy(rqb[:, :], pE[:, 0:258]),
                 waits=[tbq])
        read_psum(trq, [6])
        xfr_l = [dma("gpsimd", xfread[:, c * TW:(c + 1) * TW],
                     d_xsp[c * 128:(c + 1) * 128, i * TW:(i + 1) * TW],
                     waits=xsp_all + [xfr_guard[0]], stream="s_xf")
                 for c in range(NCD)]
        xfr = [xfr_l[-1]] * NCD
        am_tok = M2_guard[0]
        silu_last = None
        for hb in range(8):
            msl = hb % 2
            mdma_l = [dma("sync", m1r[msl][:, k * 512:(k + 1) * 512],
                          d_m1q[(hb * 8 + k) * 128:(hb * 8 + k + 1) * 128, :],
                          waits=[m1r_tok[msl]] +
                          (m1q_spills[-1:] + xsp_all
                           if (i == 0 and hb < 2) else []),
                          stream=f"s_m1r{msl}")
                      for k in range(NCD)]
            mdma = [mdma_l[-1]]
            for cc in range(4):
                ch = hb * 4 + cc
                hsl = ch % 2
                gb = gbuf if ch % 2 == 0 else gbuf2
                tb = fftmp[:, 256:512] if ch % 2 == 0 else tanh2[:, :]
                pt = pA[:, hsl * 512:hsl * 512 + 258]
                mms = [(pt, m1r[msl][:, k * 512 + cc * 128:
                                     k * 512 + (cc + 1) * 128],
                        n1ring[sl][:, k * 258:(k + 1) * 258],
                        dict(start=(k == 0), stop=(k == NCD - 1)))
                       for k in range(NCD)]
                tg = mm_group([hsl], mms,
                              extra_waits=mdma + st_toks + [hin_tok[hsl]])
                m1r_tok[msl] = tg
                n1_slot_use[sl] = tg
                th = ev("vector", lambda v, hsl=hsl, pt=pt: v.tensor_tensor(
                    h_in[hsl][:, :], pt, rqb[:, :], OP.mult),
                    waits=[tg, trq])
                read_psum(th, [hsl])
                tc1 = ev("vector", lambda g, hsl=hsl, ch=ch, gb=gb:
                         g.tensor_scalar(
                             gb[:, :], h_in[hsl][:, 1:257],
                             wcols[:, 32 + ch:33 + ch], None, OP.mult),
                         waits=[th, t_wcols, gb_tok[ch % 2]])
                tc2 = ev("vector", lambda g, hsl=hsl, ch=ch, gb=gb:
                         g.scalar_tensor_tensor(
                             gb[:, :], h_in[hsl][:, 0:256],
                             wcols[:, ch:ch + 1], gb[:, :], OP.mult, OP.add),
                         waits=[tc1])
                tc3 = ev("vector", lambda g, hsl=hsl, ch=ch, gb=gb:
                         g.scalar_tensor_tensor(
                             gb[:, :], h_in[hsl][:, 2:258],
                             wcols[:, 64 + ch:65 + ch], gb[:, :],
                             OP.mult, OP.add), waits=[tc2])
                hin_tok[hsl] = tc3
                tth = ev("scalar", lambda sc_, ch=ch, gb=gb, tb=tb:
                         sc_.activation(
                             tb, gb[:, :], AF.Tanh, scale=0.5,
                             bias=wcols[:, 96 + ch:97 + ch]),
                         waits=[tc3, tb_guard[ch % 2]])
                ta2 = ev("vector", lambda v, ch=ch, gb=gb: v.tensor_scalar(
                    hs[:, ch * TW:(ch + 1) * TW], gb[:, :], 0.5,
                    wcols[:, 96 + ch:97 + ch], OP.mult, OP.add),
                    waits=[tc3, hs_guard[0]])
                pend.append((ch, gb, tb, tth, ta2))
                if len(pend) >= 2 or ch == NCH - 1:
                    todo = pend[:] if ch == NCH - 1 else [pend.pop(0)]
                    for (ch2, gb2, tb2, tth2, ta22) in todo:
                        tsil = ev("vector",
                                  lambda v, ch2=ch2, tb2=tb2:
                                  v.scalar_tensor_tensor(
                                      hs[:, ch2 * TW:(ch2 + 1) * TW], tb2,
                                      1.0, hs[:, ch2 * TW:(ch2 + 1) * TW],
                                      OP.add, OP.mult),
                                  waits=[tth2, ta22])
                        tb_guard[ch2 % 2] = tsil
                        if ch2 % 2 == 0:
                            ffA_guard[0] = tsil
                        silu_last = tsil
                        tabs = ev("vector",
                                  lambda g, ch2=ch2, gb2=gb2:
                                  g.scalar_tensor_tensor(
                                      gb2[:, :],
                                      hs[:, ch2 * TW:(ch2 + 1) * TW], -1.0,
                                      hs[:, ch2 * TW:(ch2 + 1) * TW],
                                      OP.mult, OP.max),
                                  waits=[tsil])
                        gb_tok[ch2 % 2] = tabs
                        if ch2 == 0:
                            am_tok = ev("vector", lambda v, gb2=gb2:
                                        v.tensor_copy(M2[:, :], gb2[:, :]),
                                        waits=[tabs, M2_guard[0]])
                        else:
                            am_tok = ev("vector", lambda v, gb2=gb2:
                                        v.tensor_tensor(
                                            M2[:, :], M2[:, :], gb2[:, :],
                                            OP.max), waits=[tabs, am_tok])
                    if ch == NCH - 1:
                        pend.clear()
        red_toks = []
        for j in range(2):
            tm = mm_group([7], [(pF[:, 0:128], M2[:, j * 128:(j + 1) * 128],
                                 ident[:, :],
                                 dict(start=True, stop=True,
                                      is_transpose=True))],
                          extra_waits=[am_tok, t_id, rqp_guard[0]])
            tr = ev("vector", lambda v, j=j: v.tensor_reduce(
                scol[:, 32 + j:33 + j], pF[:, 0:128], AX.X, OP.max),
                waits=[tm], strict=True)
            read_psum(tr, [7])
            red_toks.append(tr)
        M2_guard[0] = tm
        t9 = ev("vector", lambda v: v.tensor_scalar(
            scol[:, 34:36], scol[:, 32:34], QCLIP, None, OP.max),
            waits=red_toks)
        t10 = ev("vector", lambda v: v.reciprocal(
            scol[:, 36:38], scol[:, 34:36]), waits=[t9], strict=True)
        t11 = ev("vector", lambda v: v.tensor_scalar(
            scol[:, 38:40], scol[:, 36:38], 127.0, None, OP.mult),
            waits=[t10])
        t12 = ev("vector", lambda v: v.tensor_tensor(
            scol[:, 40:41], scol[:, 34:35], cells[:, 2:3], OP.mult),
            waits=[t9, c_w2[0]])
        t12b = ev("vector", lambda v: v.tensor_tensor(
            scol[:, 41:42], scol[:, 35:36], cells[:, 2:3], OP.mult),
            waits=[t12])
        tp1 = ev("vector", lambda v: v.tensor_copy(rqpair[:, 0:2],
                                                   scol[:, 38:40]),
                 waits=[t11, rqp_guard[0]])
        tp2 = ev("vector", lambda v: v.tensor_copy(rqpair[:, 2:4],
                                                   scol[:, 40:42]),
                 waits=[t12b, tp1])
        rdma = []
        for kk in range(4):
            tm = mm_group([7], [(pF[0:1, 0:128], rqpair[:, kk:kk + 1],
                                 ident[:, :],
                                 dict(start=True, stop=True,
                                      is_transpose=True))],
                          extra_waits=[tp2, t_id])
            rqp_guard[0] = tm
            te = ev("vector", lambda v, kk=kk: v.tensor_copy(
                rows1[0:1, 2050 + kk * 128:2050 + (kk + 1) * 128],
                pF[0:1, 0:128]), waits=[tm, rowff_guard[0]])
            read_psum(te, [7])
            rdma.append(te)
        tb1 = mm_group([6], [(pE[:, 0:TW], ones[0:1, 0:128],
                              rows1[0:1, 2050:2050 + TW],
                              dict(start=True, stop=True))],
                       extra_waits=rdma + [t_ones])
        tsB = ev("vector", lambda v: v.tensor_copy(sBb[:, :], pE[:, 0:TW]),
                 waits=[tb1])
        read_psum(tsB, [6])
        tb2 = mm_group([6], [(pE[:, 0:TW], ones[0:1, 0:128],
                              rows1[0:1, 2050 + TW:2050 + 2 * TW],
                              dict(start=True, stop=True))],
                       extra_waits=rdma)
        trs = ev("vector", lambda v: v.tensor_copy(rs2b[:, :], pE[:, 0:TW]),
                 waits=[tb2])
        read_psum(trs, [6])
        rowff_guard[0] = tb2
        n2_toks = []
        for ch in range(NCH):
            t_xq = ev("vector", lambda v, ch=ch: v.tensor_tensor(
                fftmp[:, 256:512], hs[:, ch * TW:(ch + 1) * TW], sBb[:, :],
                OP.mult), waits=[tsB, silu_last, ffA_guard[0], n2_guard[0]])
            t_i = ev("vector", lambda v: v.tensor_copy(fft_i[:, :],
                                                       fftmp[:, 256:512]),
                     waits=[t_xq])
            t_n = ev("vector", lambda v, ch=ch: v.tensor_copy(
                n2[:, ch * TW:(ch + 1) * TW], fft_i[:, :]), waits=[t_i])
            ffA_guard[0] = t_n
            n2_toks.append(t_n)
        hs_guard[0] = n2_toks[-1]
        for o in range(NCD):
            msl = o % 2
            mdma = dma("sync",
                       m2r[msl][:, :],
                       d_m2q[o * 128:(o + 1) * 128, :],
                       waits=[m2r_tok[msl]] +
                       (m2q_spills[-1:] + xsp_all
                        if (i == 0 and o < 2) else []),
                       stream=f"s_m2r{msl}")
            pt = pB[:, msl * 512:msl * 512 + TW]
            mms = [(pt, m2r[msl][:, k * 128:(k + 1) * 128],
                    n2[:, k * TW:(k + 1) * TW],
                    dict(start=(k == 0), stop=(k == NCH - 1)))
                   for k in range(NCH)]
            tg = mm_group([2 + msl], mms, extra_waits=[mdma] + n2_toks)
            m2r_tok[msl] = tg
            ty = ev("vector", lambda v, pt=pt: v.tensor_tensor(
                fftmp[:, 0:256], pt, rs2b[:, :], OP.mult),
                waits=[tg, trs, ffB_guard[0]])
            read_psum(ty, [2 + msl])
            to = ev("vector", lambda v, o=o, msl=msl: v.tensor_tensor(
                otile[:, msl * 256:(msl + 1) * 256], fftmp[:, 0:256],
                xfread[:, o * TW:(o + 1) * TW], OP.add),
                waits=[ty, xfr[o], ot_tok[msl]])
            ffB_guard[0] = to
            td = dma("gpsimd", d_yT[o * 128:(o + 1) * 128,
                                    i * TW:(i + 1) * TW],
                     otile[:, msl * 256:(msl + 1) * 256], waits=[to],
                     stream=f"s_out{msl}")
            ot_tok[msl] = td
            tok_out.append(td)
        n2_guard[0] = tg
        xfr_guard[0] = to

    quant_stage(0)
    quant_stage(1)
    for idx in range(32):
        m2q_spills += wq_passB(d_w2t, d_m2q, idx, 1, False, 3,
                               extra=list(c_w2) +
                               (xsp_all if idx < 2 else []))
    for i in range(NT):
        st = stitch(i)
        if i + 2 < NT:
            quant_stage(i + 2)
        ffn_tile(i, st)
        if "n1r0" in dumps and i == 0:
            dump_sbuf("n1r0", n1ring[0][:, :], n1_tok[0])
            dump_sbuf("hs0", hs[:, :], [hs_guard[0]])
            dump_sbuf("n20", n2[:, :], [hs_guard[0]])
        if "n2t2" in dumps and i == 2:
            dump_sbuf("n1r2", n1ring[2][:, :], n1_tok[2])
            dump_sbuf("n2t2", n2[:, :], [hs_guard[0], n2_guard[0]])
            dump_sbuf("rqb2", rqb[:, :], [hs_guard[0]])
            dump_sbuf("sbb2", sBb[:, :], [hs_guard[0]])
            dump_sbuf("rsb2", rs2b[:, :], [hs_guard[0]])
            hs_guard[0] = tok_out[-1]
            n2_guard[0] = tok_out[-1]

    dump_xsp("xspend", [ot_tok[0], ot_tok[1]])
    ev("gpsimd", lambda g: g.nop(), waits=tok_out + [ot_tok[0], ot_tok[1]])

    sems = {}
    for name in sorted(set(list(s.cnt.keys()))):
        sems[name] = nc.alloc_semaphore(name=name)
    nums = sorted(h.num for h in sems.values())
    assert nums == list(range(nums[0], nums[0] + len(nums)))
    nc.gpsimd.sem_clear(range(nums[0], nums[0] + len(nums)))
    nc.all_engine_barrier()
    with nc.Block() as block:

        def make_body(eng_name):
            def body(eng):
                waited = {}
                for (w, fn, sem, inc) in s.ops[eng_name]:
                    for (ws, wv) in w:
                        if waited.get(ws, -1) >= wv:
                            continue
                        eng.wait_ge(sems[ws], wv)
                        waited[ws] = wv
                    inst = fn(eng)
                    if inst is not None:
                        inst.then_inc(sems[sem], inc)
                    else:
                        eng.nop().then_inc(sems[sem], inc)
            return body

        block.sync(make_body("sync"))
        block.tensor(make_body("tensor"))
        block.vector(make_body("vector"))
        block.scalar(make_body("scalar"))
        block.gpsimd(make_body("gpsimd"))

    return nc, dump_outs


def _prep_inputs(inputs):
    x = np.asarray(inputs["L0_attn_x"], dtype=np.float32)
    B = x.shape[0]
    xT = np.ascontiguousarray(np.transpose(x, (0, 2, 1)))
    shared = {
        "wq": np.ascontiguousarray(inputs["L0_attn_Wq"], dtype=np.float32),
        "wk": np.ascontiguousarray(inputs["L0_attn_Wk"], dtype=np.float32),
        "wv": np.ascontiguousarray(inputs["L0_attn_Wv"], dtype=np.float32),
        "wo": np.ascontiguousarray(inputs["L0_attn_Wo"], dtype=np.float32),
        "bq": np.ascontiguousarray(inputs["L0_attn_bq"],
                                   dtype=np.float32).reshape(1, D),
        "bk": np.ascontiguousarray(inputs["L0_attn_bk"],
                                   dtype=np.float32).reshape(1, D),
        "bv": np.ascontiguousarray(inputs["L0_attn_bv"],
                                   dtype=np.float32).reshape(1, D),
        "bo": np.ascontiguousarray(inputs["L0_attn_bo"],
                                   dtype=np.float32).reshape(1, D),
        "w1t": np.ascontiguousarray(
            np.asarray(inputs["L0_ffn_w1"], dtype=np.float32).T),
        "cw": np.ascontiguousarray(
            np.asarray(inputs["L0_ffn_conv_w"], dtype=np.float32)[:, 0, :].T),
        "cb": np.ascontiguousarray(inputs["L0_ffn_conv_b"],
                                   dtype=np.float32).reshape(1, DH),
        "w2t": np.ascontiguousarray(
            np.asarray(inputs["L0_ffn_w2"], dtype=np.float32).T),
    }
    return xT, shared, B


def kernel(**inputs):
    xT, shared, B = _prep_inputs(inputs)
    S = xT.shape[2]
    key = ("prog", S)
    if key not in _CACHE:
        _CACHE[key] = build_program(S, paranoid=os.environ.get(
            "K_SCHED_MODE", "semi"))
    nc, _ = _CACHE[key]
    in_maps = [dict(shared, xT=xT[b]) for b in range(B)]
    res = run_bass_kernel_spmd(nc, in_maps, core_ids=list(range(B)))
    out = np.stack([np.ascontiguousarray(
        np.asarray(res.results[b]["yT"]).T) for b in range(B)], axis=0)
    return out
